# version 04
# baseline (speedup 1.0000x reference)
import sys
sys.path.insert(0, "/opt/trn_rl_repo")

import numpy as np
from contextlib import ExitStack

import concourse.bass as bass
import concourse.mybir as mybir
from concourse.bass_utils import run_bass_kernel_spmd

F32 = mybir.dt.float32
BF16 = mybir.dt.bfloat16
AF = mybir.ActivationFunctionType
ALU = mybir.AluOpType

B, SEQ, D = 4, 4096, 1024
T = 2048
TK = 2176
H, HD = 16, 64
WND = 160
WPAD = 256
ACOLS = H * WPAD
NTCH = ACOLS // 128
NQB = 16
KD = 8
NT = 16
NTT = 32
S = 2048
NSC = 16
NE = 4
HID = 4096
NHC = 32
SCALE_ATT = 0.125
MASKVAL = -30000.0
GROUPS = [[0, 1], [2, 3], [4, 5], [6, 7]]

DEBUG = False


def build_program(upto=None):
    nc = bass.Bass("TRN2", target_bir_lowering=False, debug=False)

    x_in = nc.dram_tensor("x", [TK, D], F32, kind="ExternalInput").ap()
    mask0_in = nc.dram_tensor("mask0", [128, WND], F32, kind="ExternalInput").ap()
    maskg_in = nc.dram_tensor("maskg", [128, WND], F32, kind="ExternalInput").ap()
    wq_in = nc.dram_tensor("wq", [D, D], F32, kind="ExternalInput").ap()
    wk_in = nc.dram_tensor("wk", [D, D], F32, kind="ExternalInput").ap()
    wv_in = nc.dram_tensor("wv", [D, D], F32, kind="ExternalInput").ap()
    wo_in = nc.dram_tensor("wo", [D, D], F32, kind="ExternalInput").ap()
    bo_in = nc.dram_tensor("bo", [1, D], F32, kind="ExternalInput").ap()
    g1_in = nc.dram_tensor("g1", [128, KD], F32, kind="ExternalInput").ap()
    g2_in = nc.dram_tensor("g2", [128, KD], F32, kind="ExternalInput").ap()
    se_in = nc.dram_tensor("se", [S, D], F32, kind="ExternalInput").ap()
    w1_in = nc.dram_tensor("w1", [NE, D, HID], F32, kind="ExternalInput").ap()
    w2_in = nc.dram_tensor("w2", [NE, HID, D], F32, kind="ExternalInput").ap()
    b1_in = nc.dram_tensor("b1", [128, NHC, NE], F32, kind="ExternalInput").ap()
    b2_in = nc.dram_tensor("b2", [1, NE * D], F32, kind="ExternalInput").ap()

    out_t = nc.dram_tensor("out", [T, D], F32, kind="ExternalOutput").ap()

    xt_d = nc.dram_tensor("xt_d", [TK, D], BF16).ap()
    snt_d = nc.dram_tensor("snt_d", [S, D], BF16).ap()
    raw_d = [nc.dram_tensor(f"raw{i}_d", [SEQ, 1024], F32).ap() for i in range(2)]
    ec_d = nc.dram_tensor("ec_d", [SEQ, S], BF16).ap()
    slots_d = nc.dram_tensor("slots_d", [S, D], BF16).ap()
    y_d = nc.dram_tensor("y_d", [T, D], F32).ap()
    cmrow_d = nc.dram_tensor("cmrow_d", [16, 128], F32).ap()

    ag_in_d = nc.dram_tensor("xn_ag_in", [T, D], BF16)
    ag_out_d = nc.dram_tensor("xn_ag_out", [SEQ, D], BF16)
    mar_in_d = nc.dram_tensor("mar_in", [128, NTT], F32)
    mar_out_d = nc.dram_tensor("mar_out", [128, NTT], F32)
    prs_in_d = nc.dram_tensor("prs_in", [SEQ, 1032], F32)
    prs_out_d = nc.dram_tensor("prs_out", [T, 1032], F32)
    fence_d = [(nc.dram_tensor(f"fi{i}", [128, 8], F32), nc.dram_tensor(f"fo{i}", [128, 8], F32))
               for i in range(3)]

    dbg = {}
    if DEBUG:
        for name, shape, dt in [
            ("dbg_y", [T, D], F32), ("dbg_xnag", [SEQ, D], BF16),
            ("dbg_raw0", [SEQ, 1024], F32), ("dbg_raw1", [SEQ, 1024], F32),
            ("dbg_slots", [S, D], BF16), ("dbg_ec", [SEQ, S], BF16),
            ("dbg_prs", [T, 1032], F32), ("dbg_mar", [128, NTT], F32),
        ]:
            dbg[name] = nc.dram_tensor(name, shape, dt, kind="ExternalOutput").ap()

    ctx = ExitStack()
    with ctx:
        ARENA = 206528
        arena = ctx.enter_context(nc.sbuf_tensor("arena", [128, ARENA], mybir.dt.uint8))
        abase = nc.lookup_mloc(arena).addr

        def ovl(off, shape, dtype, name):
            h = nc.virtual_sbuf_tensor(name, shape, dtype)
            nc._tensor(h.name, shape, dtype, type="SB")
            mloc = nc.lookup_mloc(h)
            mloc.allocated = True
            mloc.addr = abase + off
            per_part = int(np.prod(shape[1:])) * mybir.dt.size(dtype)
            assert off + per_part <= ARENA, f"{name}: {off}+{per_part} > {ARENA}"
            return h

        _so = [0]
        def st(shape, dtype, name):
            h = ovl(_so[0], shape, dtype, name)
            per_part = int(np.prod(shape[1:])) * mybir.dt.size(dtype)
            _so[0] = (_so[0] + per_part + 31) & ~31
            return h
        ident_f = st([128, 128], F32, "ident_f")
        ident_b = st([128, 128], BF16, "ident_b")
        ones1 = st([1, 128], F32, "ones1")
        m320_g = st([128, 320], F32, "m320_g")
        m320_0 = st([128, 320], F32, "m320_0")
        boB = st([128, D], F32, "boB")
        b2Ball = st([128, NE, D], BF16, "b2Ball")
        g1s = st([128, KD], F32, "g1s")
        g12 = st([128, KD], F32, "g12")
        gA = st([128, KD], F32, "gA")
        gB = st([128, KD], F32, "gB")
        b1pp = st([128, NHC, NE], F32, "b1pp")
        rowmax = st([128, NTT], F32, "rowmax")
        negM = st([128, NTT], F32, "negM")
        cm_pp = st([128, NSC], F32, "cm_pp")
        sqs = st([128, D], F32, "sqs")
        se_stg = st([128, D], F32, "se_stg")
        sn_stg = st([128, D], BF16, "sn_stg")
        stat = st([128, 16], F32, "stat")
        stat2 = st([128, 16], F32, "stat2")
        stat3 = st([128, 16], F32, "stat3")
        recips = st([128, 16], F32, "recips")
        snst = st([128, 16], F32, "snst")
        snst2 = st([128, 16], F32, "snst2")
        snst3 = st([128, 16], F32, "snst3")
        D0 = (_so[0] + 63) & ~63
        assert D0 <= 41000, D0

        x_tm = ovl(D0, [128, 17, 1024], BF16, "x_tm")
        borow = ovl(D0 + 34816, [1, D], F32, "borow")
        b2row = ovl(D0 + 34816 + 4096, [1, NE * D], F32, "b2row")
        negcolB = ovl(D0 + 145920, [128, 2048], BF16, "negcolB")
        cmrow = ovl(D0 + 111104, [1, NSC, 128], F32, "cmrow")
        cm16 = ovl(D0 + 119296, [16, 128], F32, "cm16")
        wq_b = ovl(D0, [128, KD, D], BF16, "wq_b")
        wv_b = ovl(D0, [128, KD, D], BF16, "wv_b")
        A_b = ovl(D0, [128, ACOLS], F32, "A_b")
        AT_b = ovl(D0 + 16384, [128, NTCH, 128], BF16, "AT_b")
        wk_b = ovl(D0 + 16384, [128, KD, D], BF16, "wk_b")
        wo_b = ovl(D0 + 16384, [128, KD, D], BF16, "wo_b")
        x_fm = ovl(D0 + 34816, [128, KD, TK], BF16, "x_fm")
        avfm = ovl(D0 + 34816, [128, KD, T], BF16, "avfm")
        qfm = ovl(D0 + 69632, [128, KD, T], BF16, "qfm")
        kfm = ovl(D0 + 102400, [128, KD, TK], BF16, "kfm")
        vtm = ovl(D0 + 137216, [128, 17, 1024], BF16, "vtm")
        xres = ovl(D0 + 102400, [128, 2, D], F32, "xres")
        ystg = ovl(D0 + 110592, [128, 2, D], F32, "ystg")
        xnstg = ovl(D0 + 118784, [128, 2, D], BF16, "xnstg")
        z_stg = ovl(D0 + 122880, [128, 2, D], F32, "z_stg")
        xn_tm = ovl(D0, [128, NTT, 1032], BF16, "xn_tm")
        sn_fm = ovl(D0 + 66048, [128, KD, S], BF16, "sn_fm")
        cmrun = ovl(D0 + 98816, [128, 2048], F32, "cmrun")
        xf_t = ovl(D0 + 107008, [128, 2, KD, 128], BF16, "xf_t")
        raw_stg = ovl(D0 + 111104, [128, 2, 2048], F32, "raw_stg")
        E_q = [ovl(D0 + 66048 + i * 32768, [128, NTT, 512], BF16, f"E_q{i}")
               for i in range(2)]
        raw_rt = ovl(D0 + 131584, [128, 2, 512], F32, "raw_rt")
        tmp_f = ovl(D0 + 135680, [128, 2, 512], F32, "tmp_f")
        ec_stg = ovl(D0 + 139776, [128, 2, 512], BF16, "ec_stg")
        sl_stg = ovl(D0 + 141824, [128, 2, 1024], BF16, "sl_stg")
        wbuf = [ovl(D0 + i * 16384, [128, KD, 1024], BF16, f"wbuf{i}") for i in range(4)]
        slots_fm = ovl(D0 + 65536, [128, KD, S], BF16, "slots_fm")
        h_b = ovl(D0 + 98304, [128, NHC, 512], BF16, "h_b")
        eo_aug = ovl(D0 + 131072, [128, NSC, 1040], BF16, "eo_aug")
        ect = ovl(D0, [128, NSC, T], BF16, "ect")
        pstg = ovl(D0 + 65536, [128, 2, 1040], F32, "pstg")
        ytile = ovl(D0 + 73856, [128, 2, D], F32, "ytile")
        ostg = ovl(D0 + 82048, [128, 2, D], F32, "ostg")

        psA = ctx.enter_context(nc.psum_tensor("psA", [128, 2048], F32))
        psB = ctx.enter_context(nc.psum_tensor("psB", [128, 2048], F32))

        semnames = [
            "d_x", "d_w", "d_c", "d_se", "d_xt", "d_xfm", "d_sn", "d_snf",
            "d_xres", "d_y", "d_xnw", "d_xnt", "d_xf0", "d_xf1", "d_raww", "d_mar",
            "d_rawr", "d_ecw", "d_slw", "d_slf", "d_ect", "d_prw", "d_pro",
            "d_yr", "d_out", "d_wb0", "d_wb1", "d_wb2", "d_wb3",
            "c_prep", "c_gid",
            "t_bb", "v_bb",
            "t_qk", "v_qk", "t_v", "v_v",
            "t_sc", "v_msk", "s_exp", "v_nrm", "t_tr", "v_at", "t_av", "v_av",
            "t_wo", "v_ln1", "s_ln1",
            "s_snB", "v_snA", "s_snD", "v_snC",
            "cc", "t_lg", "s_lg", "v_lg",
            "t_cm", "v_cm", "t_cb", "v_cb", "d_cm",
            "v_ed", "s_ed", "t_sl", "v_sl",
            "t_h", "s_h", "t_w2", "v_eo",
            "t_p", "s_p", "v_ep", "s_ep",
            "v_misc", "s_misc",
        ]
        SM = {n: ctx.enter_context(nc.semaphore(n)) for n in semnames}

        block = ctx.enter_context(nc.Block())


        def p0_setup(which, eng):
            if which == "sync":
                eng.dma_start(out=m320_g[:, 0:160], in_=maskg_in).then_inc(SM["d_c"], 16)
                eng.dma_start(out=m320_g[:, 160:320], in_=maskg_in).then_inc(SM["d_c"], 16)
                eng.dma_start(out=m320_0[:, 0:160], in_=mask0_in).then_inc(SM["d_c"], 16)
                eng.dma_start(out=m320_0[:, 160:320], in_=mask0_in).then_inc(SM["d_c"], 16)
                eng.dma_start(out=borow[:], in_=bo_in).then_inc(SM["d_c"], 16)
                eng.dma_start(out=gA[:], in_=g1_in).then_inc(SM["d_c"], 16)
                eng.dma_start(out=gB[:], in_=g2_in).then_inc(SM["d_c"], 16)
                eng.dma_start(out=b1pp[:], in_=b1_in).then_inc(SM["d_c"], 16)
                eng.dma_start(out=b2row[:], in_=b2_in).then_inc(SM["d_c"], 16)
                eng.wait_ge(SM["d_c"], 144)
                for fi, _fo in fence_d:
                    eng.dma_start(out=fi.ap(), in_=m320_g[:, 0:8]).then_inc(SM["d_c"], 16)
                eng.wait_ge(SM["d_x"], 32)
                eng.dma_start(out=xt_d.rearrange("(n p) d -> p n d", p=128), in_=x_tm[:]
                              ).then_inc(SM["d_xt"], 16)
            if which == "gpsimd":
                eng.memset(ident_f[:], 0.0).then_inc(SM["c_gid"], 1)
                eng.memset(ident_b[:], 0.0).then_inc(SM["c_gid"], 1)
                eng.memset(ones1[:], 1.0).then_inc(SM["c_gid"], 1)
                eng.wait_ge(SM["c_gid"], 2)
                eng.affine_select(out=ident_f[:], in_=ident_f[:], compare_op=ALU.not_equal,
                                  fill=1.0, base=0, pattern=[[-1, 128]],
                                  channel_multiplier=1).then_inc(SM["c_prep"], 1)
                eng.affine_select(out=ident_b[:], in_=ident_b[:], compare_op=ALU.not_equal,
                                  fill=1.0, base=0, pattern=[[-1, 128]],
                                  channel_multiplier=1).then_inc(SM["c_prep"], 1)
                eng.dma_start(out=x_tm[:, 0:9, :], in_=x_in[0:1152, :].rearrange(
                    "(n p) d -> p n d", p=128)).then_inc(SM["d_x"], 16)
                eng.dma_start(out=x_tm[:, 9:17, :], in_=x_in[1152:2176, :].rearrange(
                    "(n p) d -> p n d", p=128)).then_inc(SM["d_x"], 16)
                eng.wait_ge(SM["d_xt"], 16)
                eng.dma_start(out=wq_b[:], in_=wq_in.rearrange("(k p) d -> p k d", p=128)
                              ).then_inc(SM["d_w"], 16)
                eng.dma_start(out=wk_b[:], in_=wk_in.rearrange("(k p) d -> p k d", p=128)
                              ).then_inc(SM["d_w"], 16)
            if which == "scalar":
                eng.wait_ge(SM["d_xt"], 16)
                eng.wait_ge(SM["t_bb"], 2)
                for c in range(KD):
                    eng.dma_start(out=x_fm[:, c, :], in_=xt_d[:, 128 * c:128 * (c + 1)],
                                  transpose=True).then_inc(SM["d_xfm"], 16)
            if which == "vector":
                eng.wait_ge(SM["d_c"], 144)
                eng.tensor_scalar(out=g1s[:], in0=gA[:], scalar1=32.0, scalar2=None, op0=ALU.mult)
                eng.tensor_tensor(out=g12[:], in0=gA[:], in1=gB[:], op=ALU.mult)
                eng.drain()
                eng.tensor_scalar(out=g12[:], in0=g12[:], scalar1=1024.0, scalar2=None,
                                  op0=ALU.mult)
                eng.drain()
                eng.nop().then_inc(SM["v_misc"], 1)
            if which == "tensor":
                eng.wait_ge(SM["c_gid"], 3)
                eng.wait_ge(SM["d_c"], 144)
                for j in range(2):
                    eng.matmul(psA[:, 512 * j:512 * (j + 1)], ones1[:],
                               borow[:, 512 * j:512 * (j + 1)], start=True, stop=True)
                for e in range(3):
                    reg = psA if e == 0 else psB
                    base = 1024 if e == 0 else 1024 * (e - 1)
                    for j in range(2):
                        mm = eng.matmul(reg[:, base + 512 * j:base + 512 * (j + 1)], ones1[:],
                                        b2row[:, 1024 * e + 512 * j:1024 * e + 512 * (j + 1)],
                                        start=True, stop=True)
                mm.then_inc(SM["t_bb"], 1)
                eng.wait_ge(SM["v_bb"], 1)
                for j in range(2):
                    mm = eng.matmul(psA[:, 512 * j:512 * (j + 1)], ones1[:],
                                    b2row[:, 3072 + 512 * j:3072 + 512 * (j + 1)],
                                    start=True, stop=True)
                mm.then_inc(SM["t_bb"], 1)

        def p0b_bcast_evac(which, eng):
            if which == "vector":
                eng.wait_ge(SM["t_bb"], 1)
                eng.tensor_copy(boB[:], psA[:, 0:1024])
                eng.tensor_copy(b2Ball[:, 0, :], psA[:, 1024:2048])
                eng.tensor_copy(b2Ball[:, 1, :], psB[:, 0:1024])
                eng.tensor_copy(b2Ball[:, 2, :], psB[:, 1024:2048]).then_inc(SM["v_bb"], 1)
                eng.wait_ge(SM["t_bb"], 2)
                eng.tensor_copy(b2Ball[:, 3, :], psA[:, 0:1024]).then_inc(SM["v_bb"], 1)

        def p1_qkv(which, eng):
            KCH = [(0, 512), (512, 512), (1024, 512), (1536, 512), (2048, 128)]
            if which == "tensor":
                eng.wait_ge(SM["v_bb"], 2)
                eng.wait_ge(SM["d_w"], 32)
                eng.wait_ge(SM["d_xfm"], 128)
                n = 0
                for do in range(KD):
                    for j in range(4):
                        ps = psA if n % 2 == 0 else psB
                        if n >= 2:
                            eng.wait_ge(SM["v_qk"], n - 1)
                        for k in range(KD):
                            mm = eng.matmul(ps[:, 0:512], wq_b[:, k, 128 * do:128 * (do + 1)],
                                            x_fm[:, k, 32 + 512 * j:32 + 512 * (j + 1)],
                                            start=(k == 0), stop=(k == KD - 1))
                        mm.then_inc(SM["t_qk"], 1)
                        n += 1
                for do in range(KD):
                    for (f0, fl) in KCH:
                        ps = psA if n % 2 == 0 else psB
                        eng.wait_ge(SM["v_qk"], n - 1)
                        for k in range(KD):
                            mm = eng.matmul(ps[:, 0:fl], wk_b[:, k, 128 * do:128 * (do + 1)],
                                            x_fm[:, k, f0:f0 + fl],
                                            start=(k == 0), stop=(k == KD - 1))
                        mm.then_inc(SM["t_qk"], 1)
                        n += 1
                eng.wait_ge(SM["d_w"], 48)
                for tn in range(17):
                    ps = psA if tn % 2 == 0 else psB
                    if tn >= 2:
                        eng.wait_ge(SM["v_v"], tn - 1)
                    for k in range(KD):
                        for j in range(2):
                            mm = eng.matmul(ps[:, 1024 + 512 * j:1024 + 512 * (j + 1)],
                                            x_fm[:, k, 128 * tn:128 * (tn + 1)],
                                            wv_b[:, k, 512 * j:512 * (j + 1)],
                                            start=(k == 0), stop=(k == KD - 1))
                    mm.then_inc(SM["t_v"], 1)
            if which == "vector":
                n = 0
                for do in range(KD):
                    for j in range(4):
                        ps = psA if n % 2 == 0 else psB
                        eng.wait_ge(SM["t_qk"], n + 1)
                        eng.tensor_scalar(out=qfm[:, do, 512 * j:512 * (j + 1)],
                                          in0=ps[:, 0:512], scalar1=SCALE_ATT, scalar2=None,
                                          op0=ALU.mult).then_inc(SM["v_qk"], 1)
                        n += 1
                for do in range(KD):
                    for (f0, fl) in [(0, 512), (512, 512), (1024, 512), (1536, 512), (2048, 128)]:
                        ps = psA if n % 2 == 0 else psB
                        eng.wait_ge(SM["t_qk"], n + 1)
                        eng.tensor_copy(kfm[:, do, f0:f0 + fl],
                                        ps[:, 0:fl]).then_inc(SM["v_qk"], 1)
                        n += 1
                for tn in range(17):
                    ps = psA if tn % 2 == 0 else psB
                    eng.wait_ge(SM["t_v"], tn + 1)
                    eng.tensor_copy(vtm[:, tn, :], ps[:, 1024:2048]).then_inc(SM["v_v"], 1)
                eng.memset(A_b[:], 0.0)
                eng.drain()
                eng.nop().then_inc(SM["v_misc"], 1)
            if which == "gpsimd":
                eng.wait_ge(SM["v_qk"], 32)
                eng.dma_start(out=wv_b[:], in_=wv_in.rearrange("(k p) d -> p k d", p=128)
                              ).then_inc(SM["d_w"], 16)
            if which == "sync":
                for i in range(NSC):
                    if i >= 1:
                        eng.wait_ge(SM["v_snC"], i)
                    eng.dma_start(out=se_stg[:], in_=se_in[128 * i:128 * (i + 1), :]
                                  ).then_inc(SM["d_se"], 16)
                    eng.wait_ge(SM["v_snC"], i + 1)
                    eng.dma_start(out=snt_d[128 * i:128 * (i + 1), :], in_=sn_stg[:]
                                  ).then_inc(SM["d_sn"], 16)
            if which == "scalar":
                for i in range(NSC):
                    eng.wait_ge(SM["d_se"], 16 * (i + 1))
                    eng.activation(sqs[:], se_stg[:], AF.Square,
                                   accum_out=snst[:, i % 16:i % 16 + 1])
                    eng.drain()
                    eng.nop().then_inc(SM["s_snB"], 1)
                    eng.wait_ge(SM["v_snA"], i + 1)
                    eng.activation(snst3[:, i % 16:i % 16 + 1], snst2[:, i % 16:i % 16 + 1],
                                   AF.Sqrt)
                    eng.drain()
                    eng.nop().then_inc(SM["s_snD"], 1)

        def p1_sn_vec(which, eng):
            if which == "vector":
                for i in range(NSC):
                    eng.wait_ge(SM["s_snB"], i + 1)
                    eng.reciprocal(snst2[:, i % 16:i % 16 + 1], snst[:, i % 16:i % 16 + 1])
                    eng.drain()
                    eng.nop().then_inc(SM["v_snA"], 1)
                    eng.wait_ge(SM["s_snD"], i + 1)
                    if i >= 1:
                        eng.wait_ge(SM["d_sn"], 16 * i)
                    eng.tensor_scalar(out=sn_stg[:], in0=se_stg[:],
                                      scalar1=snst3[:, i % 16:i % 16 + 1], scalar2=None,
                                      op0=ALU.mult)
                    eng.drain()
                    eng.nop().then_inc(SM["v_snC"], 1)

        def p2_attn(which, eng):
            def bankap(ps_pair, b, f0, fl):
                ps = ps_pair[0] if b < 4 else ps_pair[1]
                return ps[:, 512 * (b % 4) + f0:512 * (b % 4) + f0 + fl]
            if which == "tensor":
                for qb in range(NQB):
                    for G in range(2):
                        if G == 0:
                            eng.wait_ge(SM["v_av"], qb)
                            if qb == 0:
                                eng.wait_ge(SM["v_misc"], 2)
                        else:
                            eng.wait_ge(SM["s_exp"], 2 * qb + 1)
                        for b in range(8):
                            hh = 8 * G + b
                            po = 64 * (hh % 2)
                            c = hh // 2
                            mm = eng.matmul(bankap((psA, psB), b, 0, WND),
                                            qfm[po:po + 64, c, 128 * qb:128 * (qb + 1)],
                                            kfm[po:po + 64, c, 128 * qb:128 * qb + WND],
                                            start=True, stop=True)
                        mm.then_inc(SM["t_sc"], 1)
                    eng.wait_ge(SM["v_nrm"], qb + 1)
                    for cch in range(NTCH):
                        j = cch % 4
                        if cch >= 4:
                            eng.wait_ge(SM["v_at"], NTCH * qb + cch - 3)
                        eng.transpose(psB[:, 512 * j:512 * j + 128],
                                      A_b[:, 128 * cch:128 * (cch + 1)],
                                      ident_f[:]).then_inc(SM["t_tr"], 1)
                    eng.wait_ge(SM["v_at"], NTCH * (qb + 1))
                    for hh in range(H):
                        c = hh // 2
                        r0 = WPAD * hh
                        segs = []
                        r = r0
                        while r < r0 + WPAD:
                            ch, o = r // 128, r % 128
                            ln = min(128 - o, r0 + WPAD - r)
                            segs.append((ch, o, ln, r - r0))
                            r += ln
                        for si, (ch, o, ln, w0) in enumerate(segs):
                            kt0 = 128 * qb + w0
                            vch, vo = kt0 // 128, kt0 % 128
                            po = 64 * (hh % 2)
                            mm = eng.matmul(
                                (psA if c < 4 else psB)[po:po + 64,
                                                        512 * (c % 4):512 * (c % 4) + 128],
                                vtm[vo:vo + ln, vch, 64 * hh:64 * hh + 64],
                                AT_b[o:o + ln, ch, :],
                                start=(si == 0), stop=(si == len(segs) - 1))
                    mm.then_inc(SM["t_av"], 1)
            if which == "vector":
                for qb in range(NQB):
                    m = m320_0 if qb == 0 else m320_g
                    for G in range(2):
                        eng.wait_ge(SM["t_sc"], 2 * qb + G + 1)
                        for b in range(8):
                            eng.tensor_tensor(out=bankap((psA, psB), b, 0, WND),
                                              in0=bankap((psA, psB), b, 0, WND),
                                              in1=m[:, 0:WND], op=ALU.add)
                        eng.drain()
                        for b in range(8):
                            eng.tensor_reduce(stat[:, b:b + 1],
                                              bankap((psA, psB), b, 0, WND),
                                              axis=mybir.AxisListType.X, op=ALU.max)
                        eng.drain()
                        eng.tensor_reduce(stat2[:, G:G + 1], stat[:, 0:8],
                                          axis=mybir.AxisListType.X,
                                          op=ALU.max, negate=True)
                        eng.drain()
                        eng.nop().then_inc(SM["v_msk"], 1)
                    eng.wait_ge(SM["s_exp"], 2 * qb + 2)
                    eng.tensor_reduce(stat3[:, 0:16],
                                      A_b[:].rearrange("p (h w) -> p h w", w=WPAD),
                                      axis=mybir.AxisListType.X, op=ALU.add)
                    eng.drain()
                    eng.reciprocal(recips[:, 0:16], stat3[:, 0:16])
                    eng.drain()
                    for hh in range(H):
                        eng.tensor_scalar(out=A_b[:, WPAD * hh:WPAD * hh + WND],
                                          in0=A_b[:, WPAD * hh:WPAD * hh + WND],
                                          scalar1=recips[:, hh:hh + 1], scalar2=None,
                                          op0=ALU.mult)
                    eng.drain()
                    eng.nop().then_inc(SM["v_nrm"], 1)
                    for cch in range(NTCH):
                        j = cch % 4
                        eng.wait_ge(SM["t_tr"], NTCH * qb + cch + 1)
                        eng.tensor_copy(AT_b[:, cch, :],
                                        psB[:, 512 * j:512 * j + 128]).then_inc(SM["v_at"], 1)
                    eng.wait_ge(SM["t_av"], qb + 1)
                    for hh in range(H):
                        c = hh // 2
                        po = 64 * (hh % 2)
                        eng.tensor_copy(avfm[po:po + 64, c, 128 * qb:128 * (qb + 1)],
                                        (psA if c < 4 else psB)[po:po + 64,
                                                                512 * (c % 4):512 * (c % 4) + 128])
                    eng.drain()
                    eng.nop().then_inc(SM["v_av"], 1)
            if which == "scalar":
                for qb in range(NQB):
                    A3 = A_b[:].rearrange("p (h w) -> p h w", w=WPAD)
                    for G in range(2):
                        eng.wait_ge(SM["v_msk"], 2 * qb + G + 1)
                        for b in range(8):
                            eng.activation(A3[:, 8 * G + b, 0:WND],
                                           bankap((psA, psB), b, 0, WND),
                                           AF.Exp, bias=stat2[:, G:G + 1], scale=1.0)
                        eng.drain()
                        eng.nop().then_inc(SM["s_exp"], 1)

        def p3_ln1(which, eng):
            if which == "gpsimd":
                eng.wait_ge(SM["v_av"], NQB)
                eng.dma_start(out=wo_b[:], in_=wo_in.rearrange("(k p) d -> p k d", p=128)
                              ).then_inc(SM["d_w"], 16)
            if which == "sync":
                for t in range(NT):
                    if t == 0:
                        eng.wait_ge(SM["t_sc"], 2 * NQB)
                    if t >= 2:
                        eng.wait_ge(SM["v_ln1"], 2 * (t - 1))
                    eng.dma_start(out=xres[:, t % 2, :],
                                  in_=x_in[32 + 128 * t:32 + 128 * (t + 1), :]
                                  ).then_inc(SM["d_xres"], 16)
                    eng.wait_ge(SM["s_misc"], t + 1)
                    eng.dma_start(out=y_d[128 * t:128 * (t + 1), :], in_=ystg[:, t % 2, :]
                                  ).then_inc(SM["d_y"], 16)
                    eng.dma_start(out=ag_in_d.ap()[128 * t:128 * (t + 1), :],
                                  in_=xnstg[:, t % 2, :]).then_inc(SM["d_xnw"], 16)
            if which == "tensor":
                eng.wait_ge(SM["d_w"], 64)
                for t in range(NT):
                    ps = psA if t % 2 == 0 else psB
                    if t >= 2:
                        eng.wait_ge(SM["v_ln1"], 2 * (t - 1))
                    for k in range(KD):
                        for j in range(2):
                            mm = eng.matmul(ps[:, 512 * j:512 * (j + 1)],
                                            avfm[:, k, 128 * t:128 * (t + 1)],
                                            wo_b[:, k, 512 * j:512 * (j + 1)],
                                            start=(k == 0), stop=(k == KD - 1))
                    mm.then_inc(SM["t_wo"], 1)
            if which == "vector":
                for t in range(NT):
                    ps = psA if t % 2 == 0 else psB
                    eng.wait_ge(SM["t_wo"], t + 1)
                    eng.wait_ge(SM["d_xres"], 16 * (t + 1))
                    if t >= 2:
                        eng.wait_ge(SM["s_misc"], t - 1)
                    eng.tensor_tensor(out=z_stg[:, t % 2, :], in0=ps[:, 0:1024],
                                      in1=xres[:, t % 2, :], op=ALU.add)
                    eng.drain()
                    eng.tensor_tensor(out=z_stg[:, t % 2, :], in0=z_stg[:, t % 2, :],
                                      in1=boB[:], op=ALU.add)
                    eng.drain()
                    eng.tensor_reduce(stat[:, 0:1], z_stg[:, t % 2, :],
                                      axis=mybir.AxisListType.X, op=ALU.add)
                    eng.drain()
                    eng.tensor_scalar(out=stat[:, 1:2], in0=stat[:, 0:1], scalar1=1.0 / D,
                                      scalar2=None, op0=ALU.mult)
                    eng.drain()
                    eng.tensor_scalar(out=z_stg[:, t % 2, :], in0=z_stg[:, t % 2, :],
                                      scalar1=stat[:, 1:2], scalar2=None, op0=ALU.subtract)
                    eng.drain()
                    eng.nop().then_inc(SM["v_ln1"], 1)
                    eng.wait_ge(SM["s_ln1"], t + 1)
                    eng.tensor_scalar(out=stat[:, 3:4], in0=stat[:, 2:3], scalar1=1.0 / D,
                                      scalar2=1e-5, op0=ALU.mult, op1=ALU.add)
                    eng.drain()
                    eng.reciprocal(stat[:, 4:5], stat[:, 3:4])
                    eng.reciprocal(stat[:, 5:6], stat[:, 2:3])
                    eng.drain()
                    eng.nop().then_inc(SM["v_ln1"], 1)
            if which == "scalar":
                for t in range(NT):
                    eng.wait_ge(SM["v_ln1"], 2 * t + 1)
                    eng.activation(sqs[:], z_stg[:, t % 2, :], AF.Square,
                                   accum_out=stat[:, 2:3])
                    eng.drain()
                    eng.nop().then_inc(SM["s_ln1"], 1)
                    eng.wait_ge(SM["v_ln1"], 2 * t + 2)
                    if t >= 2:
                        eng.wait_ge(SM["d_y"], 16 * t)
                        eng.wait_ge(SM["d_xnw"], 16 * t)
                    eng.activation(stat[:, 6:7], stat[:, 4:5], AF.Sqrt)
                    eng.activation(stat[:, 7:8], stat[:, 5:6], AF.Sqrt)
                    eng.drain()
                    eng.activation(ystg[:, t % 2, :], z_stg[:, t % 2, :], AF.Copy,
                                   scale=stat[:, 6:7])
                    eng.activation(xnstg[:, t % 2, :], z_stg[:, t % 2, :], AF.Copy,
                                   scale=stat[:, 7:8])
                    eng.drain()
                    eng.nop().then_inc(SM["s_misc"], 1)

        def p4_ag(which, eng):
            if which == "gpsimd":
                eng.wait_ge(SM["d_xnw"], 16 * NT)
                eng.wait_ge(SM["d_y"], 16 * NT)
                eng.wait_ge(SM["d_xres"], 16 * NT)
                eng.wait_ge(SM["d_c"], 192)
                eng.collective_compute("AllGather", ALU.bypass, replica_groups=GROUPS,
                                       ins=[ag_in_d.ap().opt()], outs=[ag_out_d.ap().opt()]
                                       ).then_inc(SM["cc"], 1)
                eng.collective_compute("AllReduce", ALU.add, replica_groups=GROUPS,
                                       ins=[fence_d[0][0].ap().opt()],
                                       outs=[fence_d[0][1].ap().opt()]).then_inc(SM["cc"], 1)
            if which == "sync":
                eng.wait_ge(SM["cc"], 2)
                for g4 in range(4):
                    eng.dma_start(out=xn_tm[:, 8 * g4:8 * (g4 + 1), 0:1024],
                                  in_=ag_out_d.ap()[1024 * g4:1024 * (g4 + 1), :].rearrange(
                                      "(n p) d -> p n d", p=128)).then_inc(SM["d_xnt"], 16)
            if which == "scalar":
                eng.wait_ge(SM["d_sn"], 16 * NSC)
                eng.wait_ge(SM["d_xnw"], 16 * NT)
                for c in range(KD):
                    eng.dma_start(out=sn_fm[:, c, :], in_=snt_d[:, 128 * c:128 * (c + 1)],
                                  transpose=True).then_inc(SM["d_snf"], 16)
            if which == "vector":
                eng.wait_ge(SM["d_xnt"], 64)
                eng.memset(xn_tm[:, :, 1024:1025], 1.0)
                eng.memset(xn_tm[:, :, 1025:1032], 0.0)
                eng.drain()
                eng.nop().then_inc(SM["v_misc"], 1)
                eng.wait_ge(SM["d_snf"], 128)
                for c in range(KD):
                    eng.tensor_scalar(out=sn_fm[:, c, :], in0=sn_fm[:, c, :],
                                      scalar1=g12[:, c:c + 1], scalar2=None, op0=ALU.mult)
                eng.drain()
                eng.nop().then_inc(SM["v_misc"], 1)

        def p5_logits(which, eng):
            if which == "scalar":
                for t in range(NTT + 2):
                    if t >= 2:
                        tt = t - 2
                        ps = psA if tt % 2 == 0 else psB
                        eng.wait_ge(SM["t_lg"], tt + 1)
                        if tt >= 2:
                            eng.wait_ge(SM["d_raww"], 32 * tt)
                            eng.wait_ge(SM["v_lg"], 2 * (tt - 1))
                        eng.copy(raw_stg[:, tt % 2, :], ps[:, :])
                        eng.drain()
                        eng.nop().then_inc(SM["s_lg"], 1)
                    if t < NTT:
                        eng.wait_ge(SM["cc"], 2)
                        if t >= 2:
                            eng.wait_ge(SM["t_lg"], t - 1)
                        for c in range(KD):
                            eng.dma_start(out=xf_t[:, t % 2, c, :],
                                          in_=ag_out_d.ap()[128 * t:128 * (t + 1),
                                                            128 * c:128 * (c + 1)],
                                          transpose=True).then_inc(SM[f"d_xf{t % 2}"], 16)
            if which == "tensor":
                eng.wait_ge(SM["v_misc"], 4)
                for t in range(NTT):
                    ps = psA if t % 2 == 0 else psB
                    eng.wait_ge(SM[f"d_xf{t % 2}"], 16 * KD * (t // 2 + 1))
                    if t >= 2:
                        eng.wait_ge(SM["s_lg"], t - 1)
                    for k in range(KD):
                        for sc in range(4):
                            mm = eng.matmul(ps[:, 512 * sc:512 * (sc + 1)],
                                            xf_t[:, t % 2, k, :],
                                            sn_fm[:, k, 512 * sc:512 * (sc + 1)],
                                            start=(k == 0), stop=(k == KD - 1))
                    mm.then_inc(SM["t_lg"], 1)
            if which == "sync":
                for t in range(NTT):
                    eng.wait_ge(SM["s_lg"], t + 1)
                    eng.dma_start(out=raw_d[0][128 * t:128 * (t + 1), :],
                                  in_=raw_stg[:, t % 2, 0:1024]).then_inc(SM["d_raww"], 16)
                    eng.dma_start(out=raw_d[1][128 * t:128 * (t + 1), :],
                                  in_=raw_stg[:, t % 2, 1024:2048]).then_inc(SM["d_raww"], 16)
                eng.wait_ge(SM["v_lg"], 2 * NTT)
                eng.dma_start(out=mar_in_d.ap(), in_=rowmax[:]).then_inc(SM["d_mar"], 16)
                eng.wait_ge(SM["v_cm"], NSC + 1)
                eng.dma_start(out=cmrow_d, in_=cm16[:]).then_inc(SM["d_cm"], 16)
                eng.wait_ge(SM["d_cm"], 16)
                eng.dma_start(out=cmrow[:],
                              in_=cmrow_d.rearrange("(x a) b -> x a b", x=1)
                              ).then_inc(SM["d_cm"], 16)
                eng.wait_ge(SM["cc"], 4)
                eng.dma_start(out=negM[:], in_=mar_out_d.ap()).then_inc(SM["d_mar"], 32)
            if which == "vector":
                eng.memset(cmrun[:], -1e30)
                eng.drain()
                for t in range(NTT):
                    eng.wait_ge(SM["s_lg"], t + 1)
                    eng.tensor_reduce(rowmax[:, t:t + 1], raw_stg[:, t % 2, :],
                                      axis=mybir.AxisListType.X, op=ALU.max)
                    eng.drain()
                    eng.nop().then_inc(SM["v_lg"], 1)
                    eng.tensor_tensor(out=cmrun[:], in0=cmrun[:], in1=raw_stg[:, t % 2, :],
                                      op=ALU.max)
                    eng.drain()
                    eng.nop().then_inc(SM["v_lg"], 1)
            if which == "gpsimd":
                eng.wait_ge(SM["d_mar"], 16)
                eng.collective_compute("AllReduce", ALU.max, replica_groups=GROUPS,
                                       ins=[mar_in_d.ap().opt()], outs=[mar_out_d.ap().opt()]
                                       ).then_inc(SM["cc"], 1)
                eng.collective_compute("AllReduce", ALU.add, replica_groups=GROUPS,
                                       ins=[fence_d[1][0].ap().opt()],
                                       outs=[fence_d[1][1].ap().opt()]).then_inc(SM["cc"], 1)

        def p5b_colmax(which, eng):
            if which == "tensor":
                eng.wait_ge(SM["v_lg"], 2 * NTT)
                for c in range(NSC):
                    j = c % 4
                    if c >= 4:
                        eng.wait_ge(SM["v_cm"], c - 3)
                    eng.transpose(psA[:, 512 * j:512 * j + 128],
                                  cmrun[:, 128 * c:128 * (c + 1)],
                                  ident_f[:]).then_inc(SM["t_cm"], 1)
                eng.wait_ge(SM["v_cm"], NSC)
                eng.transpose(psB[0:16, 0:128], cm_pp[:],
                              ident_f[:]).then_inc(SM["t_cm"], 1)
                eng.wait_ge(SM["d_cm"], 32)
                for c in range(NSC):
                    j = c % 4
                    if c >= 4:
                        eng.wait_ge(SM["v_cb"], c - 3)
                    eng.matmul(psA[:, 512 * j:512 * j + 128], ones1[:],
                               cmrow[:, c, :], start=True,
                               stop=True).then_inc(SM["t_cb"], 1)
            if which == "vector":
                for c in range(NSC):
                    j = c % 4
                    eng.wait_ge(SM["t_cm"], c + 1)
                    eng.tensor_reduce(cm_pp[:, c:c + 1], psA[:, 512 * j:512 * j + 128],
                                      axis=mybir.AxisListType.X, op=ALU.max)
                    eng.drain()
                    eng.nop().then_inc(SM["v_cm"], 1)
                eng.wait_ge(SM["t_cm"], NSC + 1)
                eng.wait_ge(SM["d_raww"], 32 * NTT)
                eng.tensor_copy(cm16[:], psB[0:16, 0:128]).then_inc(SM["v_cm"], 1)
                for c in range(NSC):
                    j = c % 4
                    eng.wait_ge(SM["t_cb"], c + 1)
                    eng.tensor_scalar(out=negcolB[:, 128 * c:128 * (c + 1)],
                                      in0=psA[:, 512 * j:512 * j + 128],
                                      scalar1=-1.0, scalar2=None,
                                      op0=ALU.mult).then_inc(SM["v_cb"], 1)
                if True:
                    eng.wait_ge(SM["d_mar"], 48)
                    eng.tensor_scalar(out=negM[:], in0=negM[:], scalar1=-1.0, scalar2=None,
                                      op0=ALU.mult)
                    eng.drain()
                    eng.nop().then_inc(SM["v_misc"], 1)

        def p6_eslots(which, eng):
            if which == "sync":
                for q in range(4):
                    Hh, qq = q // 2, q % 2
                    for t in range(NTT):
                        idx = q * NTT + t
                        if idx >= 2:
                            eng.wait_ge(SM["s_ed"], 2 * (idx - 1))
                            eng.wait_ge(SM["v_ed"], idx - 1)
                        if q == 0 and t == 0:
                            eng.wait_ge(SM["d_raww"], 32 * NTT)
                        eng.dma_start(out=raw_rt[:, idx % 2, :],
                                      in_=raw_d[Hh][128 * t:128 * (t + 1),
                                                    512 * qq:512 * (qq + 1)]
                                      ).then_inc(SM["d_rawr"], 16)
                        eng.wait_ge(SM["s_ed"], 2 * idx + 2)
                        eng.dma_start(out=ec_d[128 * t:128 * (t + 1), 512 * q:512 * (q + 1)],
                                      in_=ec_stg[:, idx % 2, :]).then_inc(SM["d_ecw"], 16)
                    for scc in range(4):
                        g = 4 * q + scc
                        eng.wait_ge(SM["v_sl"], g + 1)
                        eng.dma_start(out=slots_d[128 * g:128 * (g + 1), :],
                                      in_=sl_stg[:, g % 2, :]).then_inc(SM["d_slw"], 16)
            if which == "vector":
                eng.wait_ge(SM["v_cb"], NSC)
                for q in range(4):
                    for t in range(NTT):
                        idx = q * NTT + t
                        eng.wait_ge(SM["d_rawr"], 16 * (idx + 1))
                        if idx >= 2:
                            eng.wait_ge(SM["s_ed"], 2 * idx - 3)
                        eng.scalar_tensor_tensor(out=tmp_f[:, idx % 2, :],
                                                 in0=raw_rt[:, idx % 2, :], scalar=1.0,
                                                 in1=negcolB[:, 512 * q:512 * (q + 1)],
                                                 op0=ALU.mult, op1=ALU.add)
                        eng.drain()
                        eng.nop().then_inc(SM["v_ed"], 1)
                    for scc in range(4):
                        g = 4 * q + scc
                        ps = psA if g % 2 == 0 else psB
                        eng.wait_ge(SM["t_sl"], g + 1)
                        if g >= 2:
                            eng.wait_ge(SM["d_slw"], 16 * g)
                        eng.reciprocal(stat2[:, 1:2], ps[:, 1024:1025])
                        eng.drain()
                        eng.tensor_scalar(out=sl_stg[:, g % 2, :], in0=ps[:, 0:1024],
                                          scalar1=stat2[:, 1:2], scalar2=None, op0=ALU.mult)
                        eng.drain()
                        eng.nop().then_inc(SM["v_sl"], 1)
            if which == "scalar":
                for q in range(4):
                    if q >= 2:
                        eng.wait_ge(SM["t_sl"], 4 * (q - 1))
                    for t in range(NTT):
                        idx = q * NTT + t
                        eng.wait_ge(SM["v_ed"], idx + 1)
                        eng.activation(E_q[q % 2][:, t, :], tmp_f[:, idx % 2, :],
                                       AF.Exp).then_inc(SM["s_ed"], 1)
                        eng.wait_ge(SM["v_misc"], 5)
                        if idx >= 2:
                            eng.wait_ge(SM["d_ecw"], 16 * idx)
                        eng.activation(ec_stg[:, idx % 2, :], raw_rt[:, idx % 2, :], AF.Exp,
                                       bias=negM[:, t:t + 1], scale=1.0)
                        eng.drain()
                        eng.nop().then_inc(SM["s_ed"], 1)
            if which == "tensor":
                eng.wait_ge(SM["v_misc"], 3)
                for q in range(4):
                    for scc in range(4):
                        g = 4 * q + scc
                        ps = psA if g % 2 == 0 else psB
                        if g >= 2:
                            eng.wait_ge(SM["v_sl"], g - 1)
                        for t in range(NTT):
                            idx = q * NTT + t
                            eng.wait_ge(SM["s_ed"], 2 * idx + 1)
                            for (f0, fl) in [(0, 512), (512, 512), (1024, 8)]:
                                mm = eng.matmul(ps[:, f0:f0 + fl],
                                                E_q[q % 2][:, t, 128 * scc:128 * (scc + 1)],
                                                xn_tm[:, t, f0:f0 + fl],
                                                start=(t == 0), stop=(t == NTT - 1))
                        mm.then_inc(SM["t_sl"], 1)

        def p7_ffn(which, eng):
            def wl(i):
                e, r = divmod(i, 12)
                if r < 4:
                    return True, e, 0, r
                sp, jq = divmod(r - 4, 4)
                return False, e, sp, jq
            NL = 12 * NE
            if which == "gpsimd":
                eng.wait_ge(SM["t_sl"], NSC)
                for i in range(NL):
                    isw1, e, sp, jq = wl(i)
                    if i >= 4:
                        pw1, pe, psp, pjq = wl(i - 4)
                        if pw1:
                            eng.wait_ge(SM["t_h"], NHC * pe + 8 * (pjq + 1))
                        else:
                            eng.wait_ge(SM["t_w2"], 8 * pe + 4 * psp + pjq + 1)
                    if isw1:
                        src = w1_in[e, :, 1024 * jq:1024 * (jq + 1)].rearrange(
                            "(k p) h -> p k h", p=128)
                    else:
                        src = w2_in[e, 1024 * jq:1024 * (jq + 1), :].rearrange(
                            "(k p) d -> p k d", p=128)
                    eng.dma_start(out=wbuf[i % 4][:], in_=src).then_inc(SM[f"d_wb{i % 4}"], 16)
            if which == "scalar":
                eng.wait_ge(SM["d_slw"], 16 * NSC)
                for c in range(KD):
                    eng.dma_start(out=slots_fm[:, c, :], in_=slots_d[:, 128 * c:128 * (c + 1)],
                                  transpose=True).then_inc(SM["d_slf"], 16)
                for e in range(NE):
                    for hc in range(NHC):
                        i = e * NHC + hc
                        eng.wait_ge(SM["t_h"], i + 1)
                        eng.activation(h_b[:, hc, :],
                                       psB[:, 1024 + 512 * (i % 2):1536 + 512 * (i % 2)],
                                       AF.Gelu, bias=b1pp[:, hc, e:e + 1], scale=1.0)
                        eng.drain()
                        eng.nop().then_inc(SM["s_h"], 1)
            if which == "vector":
                eng.wait_ge(SM["d_slf"], 16 * KD)
                eng.wait_ge(SM["v_misc"], 1)
                for c in range(KD):
                    eng.tensor_scalar(out=slots_fm[:, c, :], in0=slots_fm[:, c, :],
                                      scalar1=g1s[:, c:c + 1], scalar2=None, op0=ALU.mult)
                eng.drain()
                eng.wait_ge(SM["d_ecw"], 16 * 4 * NTT)
                eng.wait_ge(SM["d_rawr"], 16 * 4 * NTT)
                eng.memset(eo_aug[:, :, 1024:1040], 0.0)
                eng.drain()
                eng.memset(eo_aug[:, :, 1024:1025], 1.0)
                eng.drain()
                eng.nop().then_inc(SM["v_misc"], 2)
                eng.wait_ge(SM["v_bb"], 2)
                for e in range(NE):
                    for sp in range(2):
                        i = 2 * e + sp
                        eng.wait_ge(SM["t_w2"], 8 * e + 4 * (sp + 1))
                        for sl2 in range(2):
                            sl = 2 * sp + sl2
                            eng.tensor_tensor(out=eo_aug[:, 4 * e + sl, 0:1024],
                                              in0=psA[:, 1024 * sl2:1024 * (sl2 + 1)],
                                              in1=b2Ball[:, e, :], op=ALU.add)
                        eng.drain()
                        eng.nop().then_inc(SM["v_eo"], 1)
            if which == "tensor":
                eng.wait_ge(SM["v_misc"], 7)
                for e in range(NE):
                    if e > 0:
                        eng.wait_ge(SM["v_eo"], 2 * e)
                    else:
                        eng.wait_ge(SM["v_sl"], NSC)
                    for hc in range(NHC):
                        i = e * NHC + hc
                        jq = hc // 8
                        li = e * 12 + jq
                        eng.wait_ge(SM[f"d_wb{li % 4}"], 16 * (li // 4 + 1))
                        if i >= 2:
                            eng.wait_ge(SM["s_h"], i - 1)
                        for k in range(KD):
                            mm = eng.matmul(psB[:, 1024 + 512 * (i % 2):1536 + 512 * (i % 2)],
                                            wbuf[li % 4][:, k, 128 * (hc % 8):128 * (hc % 8 + 1)],
                                            slots_fm[:, k, 512 * e:512 * (e + 1)],
                                            start=(k == 0), stop=(k == KD - 1))
                        mm.then_inc(SM["t_h"], 1)
                    for sp in range(2):
                        i2 = 2 * e + sp
                        eng.wait_ge(SM["v_eo"], i2)
                        if e == 0 and sp == 0:
                            eng.wait_ge(SM["v_sl"], NSC)
                        for jq in range(4):
                            li = e * 12 + 4 + sp * 4 + jq
                            eng.wait_ge(SM[f"d_wb{li % 4}"], 16 * (li // 4 + 1))
                            eng.wait_ge(SM["s_h"], e * NHC + 8 * (jq + 1))
                            for sl2 in range(2):
                                sl = 2 * sp + sl2
                                for k8 in range(KD):
                                    for j in range(2):
                                        mm = eng.matmul(psA[:, 1024 * sl2 + 512 * j:
                                                            1024 * sl2 + 512 * (j + 1)],
                                                        h_b[:, 8 * jq + k8,
                                                            128 * sl:128 * (sl + 1)],
                                                        wbuf[li % 4][:, k8,
                                                                     512 * j:512 * (j + 1)],
                                                        start=(jq == 0 and k8 == 0),
                                                        stop=(jq == 3 and k8 == KD - 1))
                            mm.then_inc(SM["t_w2"], 1)

        def p8_P(which, eng):
            if which == "scalar":
                eng.wait_ge(SM["d_ecw"], 16 * 4 * NTT)
                for half in range(2):
                    eng.wait_ge(SM["t_w2"], 8 * NE)
                    if half == 1:
                        eng.wait_ge(SM["t_p"], 16)
                    for sc in range(NSC):
                        eng.dma_start(out=ect[:, sc, :],
                                      in_=ec_d[2048 * half:2048 * (half + 1),
                                               128 * sc:128 * (sc + 1)],
                                      transpose=True).then_inc(SM["d_ect"], 16)
                    for tt in range(16):
                        t = 16 * half + tt
                        ps = psA if t % 2 == 0 else psB
                        eng.wait_ge(SM["t_p"], t + 1)
                        if t >= 2:
                            eng.wait_ge(SM["d_prw"], 16 * t)
                        eng.copy(pstg[:, t % 2, 0:1040], ps[:, 0:1040])
                        eng.drain()
                        eng.nop().then_inc(SM["s_p"], 1)
            if which == "tensor":
                eng.wait_ge(SM["v_eo"], 2 * NE)
                for half in range(2):
                    eng.wait_ge(SM["d_ect"], 16 * NSC * (half + 1))
                    for tt in range(16):
                        t = 16 * half + tt
                        ps = psA if t % 2 == 0 else psB
                        if t >= 2:
                            eng.wait_ge(SM["s_p"], t - 1)
                        for sc in range(NSC):
                            for (f0, fl) in [(0, 512), (512, 512), (1024, 16)]:
                                mm = eng.matmul(ps[:, f0:f0 + fl],
                                                ect[:, sc, 128 * tt:128 * (tt + 1)],
                                                eo_aug[:, sc, f0:f0 + fl],
                                                start=(sc == 0), stop=(sc == NSC - 1))
                        mm.then_inc(SM["t_p"], 1)
            if which == "sync":
                for t in range(NTT):
                    eng.wait_ge(SM["s_p"], t + 1)
                    eng.dma_start(out=prs_in_d.ap()[128 * t:128 * (t + 1), :],
                                  in_=pstg[:, t % 2, 0:1032]).then_inc(SM["d_prw"], 16)
            if which == "gpsimd":
                eng.wait_ge(SM["d_prw"], 16 * NTT)
                eng.collective_compute("ReduceScatter", ALU.add, replica_groups=GROUPS,
                                       ins=[prs_in_d.ap().opt()], outs=[prs_out_d.ap().opt()]
                                       ).then_inc(SM["cc"], 1)
                eng.collective_compute("AllReduce", ALU.add, replica_groups=GROUPS,
                                       ins=[fence_d[2][0].ap().opt()],
                                       outs=[fence_d[2][1].ap().opt()]).then_inc(SM["cc"], 1)

        def p9_epi(which, eng):
            if which == "sync":
                eng.wait_ge(SM["cc"], 6)
                for t in range(NT):
                    if t >= 2:
                        eng.wait_ge(SM["v_ep"], 2 * (t - 1))
                    eng.dma_start(out=pstg[:, t % 2, 0:1032],
                                  in_=prs_out_d.ap()[128 * t:128 * (t + 1), :]
                                  ).then_inc(SM["d_pro"], 16)
                    eng.dma_start(out=ytile[:, t % 2, :], in_=y_d[128 * t:128 * (t + 1), :]
                                  ).then_inc(SM["d_yr"], 16)
                    eng.wait_ge(SM["s_ep"], t + 1)
                    eng.dma_start(out=out_t[128 * t:128 * (t + 1), :], in_=ostg[:, t % 2, :]
                                  ).then_inc(SM["d_out"], 16)
                nout = 16 * NT
                if DEBUG:
                    eng.wait_ge(SM["d_out"], nout)
                    for nm, src in [("dbg_y", y_d), ("dbg_xnag", ag_out_d.ap()),
                                    ("dbg_raw0", raw_d[0]), ("dbg_raw1", raw_d[1]),
                                    ("dbg_slots", slots_d), ("dbg_ec", ec_d),
                                    ("dbg_prs", prs_out_d.ap()), ("dbg_mar", mar_out_d.ap())]:
                        eng.dma_start(out=dbg[nm], in_=src).then_inc(SM["d_out"], 16)
                    nout += 16 * 8
                eng.wait_ge(SM["d_out"], nout)
            if which == "vector":
                for t in range(NT):
                    eng.wait_ge(SM["d_pro"], 16 * (t + 1))
                    eng.wait_ge(SM["d_yr"], 16 * (t + 1))
                    if t >= 2:
                        eng.wait_ge(SM["s_misc"], NT + 2 * (t - 1))
                    eng.reciprocal(stat2[:, 2:3], pstg[:, t % 2, 1024:1025])
                    eng.drain()
                    eng.scalar_tensor_tensor(out=z_stg[:, t % 2, :],
                                             in0=pstg[:, t % 2, 0:1024],
                                             scalar=stat2[:, 2:3], in1=ytile[:, t % 2, :],
                                             op0=ALU.mult, op1=ALU.add)
                    eng.drain()
                    eng.tensor_reduce(stat[:, 8:9], z_stg[:, t % 2, :],
                                      axis=mybir.AxisListType.X, op=ALU.add)
                    eng.drain()
                    eng.tensor_scalar(out=stat[:, 9:10], in0=stat[:, 8:9], scalar1=1.0 / D,
                                      scalar2=None, op0=ALU.mult)
                    eng.drain()
                    eng.tensor_scalar(out=z_stg[:, t % 2, :], in0=z_stg[:, t % 2, :],
                                      scalar1=stat[:, 9:10], scalar2=None, op0=ALU.subtract)
                    eng.drain()
                    eng.nop().then_inc(SM["v_ep"], 1)
                    eng.wait_ge(SM["s_misc"], NT + 2 * t + 1)
                    eng.tensor_scalar(out=stat[:, 11:12], in0=stat[:, 10:11], scalar1=1.0 / D,
                                      scalar2=1e-5, op0=ALU.mult, op1=ALU.add)
                    eng.drain()
                    eng.reciprocal(stat[:, 12:13], stat[:, 11:12])
                    eng.drain()
                    eng.nop().then_inc(SM["v_ep"], 1)
            if which == "scalar":
                for t in range(NT):
                    eng.wait_ge(SM["v_ep"], 2 * t + 1)
                    eng.activation(sqs[:], z_stg[:, t % 2, :], AF.Square,
                                   accum_out=stat[:, 10:11])
                    eng.drain()
                    eng.nop().then_inc(SM["s_misc"], 1)
                    eng.wait_ge(SM["v_ep"], 2 * t + 2)
                    if t >= 2:
                        eng.wait_ge(SM["d_out"], 16 * t)
                    eng.activation(stat[:, 13:14], stat[:, 12:13], AF.Sqrt)
                    eng.drain()
                    eng.activation(ostg[:, t % 2, :], z_stg[:, t % 2, :], AF.Copy,
                                   scale=stat[:, 13:14])
                    eng.drain()
                    eng.nop().then_inc(SM["s_ep"], 1)
                    eng.nop().then_inc(SM["s_misc"], 1)

        PHASES = [p0_setup, p0b_bcast_evac, p1_qkv, p1_sn_vec, p2_attn,
                  p3_ln1, p4_ag, p5_logits, p5b_colmax, p6_eslots, p7_ffn,
                  p8_P, p9_epi]
        if upto is not None:
            PHASES = PHASES[:upto]

            def p_dbg_out(which, eng):
                if which == "sync":
                    for t in range(NT):
                        eng.dma_start(out=out_t[128 * t:128 * (t + 1), :],
                                      in_=sqs[:]).then_inc(SM["d_out"], 16)
                    eng.wait_ge(SM["d_out"], 16 * NT)
            PHASES = PHASES + [p_dbg_out]

        @block.tensor
        def _(t):
            for ph in PHASES:
                ph("tensor", t)

        @block.vector
        def _(v):
            for ph in PHASES:
                ph("vector", v)

        @block.scalar
        def _(s):
            for ph in PHASES:
                ph("scalar", s)

        @block.gpsimd
        def _(g):
            for ph in PHASES:
                ph("gpsimd", g)

        @block.sync
        def _(y):
            for ph in PHASES:
                ph("sync", y)

    return nc



def _shard_inputs(inputs):
    x = np.asarray(inputs["L0_attn_x"], np.float32)
    wq = np.ascontiguousarray(inputs["L0_attn_Wq"], dtype=np.float32)
    wk = np.ascontiguousarray(inputs["L0_attn_Wk"], dtype=np.float32)
    wv = np.ascontiguousarray(inputs["L0_attn_Wv"], dtype=np.float32)
    wo = np.ascontiguousarray(inputs["L0_attn_Wo"], dtype=np.float32)
    bo = np.ascontiguousarray(np.asarray(inputs["L0_attn_bo"], np.float32).reshape(1, D))
    g1 = np.ascontiguousarray(np.asarray(inputs["L0_ffn_norm_gamma"], np.float32)
                              .reshape(KD, 128).T)
    g2 = np.ascontiguousarray(np.asarray(inputs["L0_ffn_slot_norm_gamma"], np.float32)
                              .reshape(KD, 128).T)
    se = np.asarray(inputs["L0_ffn_slot_embeds"], np.float32)
    w1 = np.asarray(inputs["L0_ffn_w1"], np.float32)
    w2 = np.asarray(inputs["L0_ffn_w2"], np.float32)
    b1 = np.asarray(inputs["L0_ffn_b1"], np.float32)
    b2 = np.asarray(inputs["L0_ffn_b2"], np.float32)

    ii = np.arange(128)[:, None]
    ww = np.arange(WND)[None, :]
    band = (ww >= ii) & (ww <= ii + 32)
    m_g = np.where(band, 0.0, MASKVAL).astype(np.float32)
    m_0h0 = np.where(band & (ww >= 32), 0.0, MASKVAL).astype(np.float32)

    in_maps = []
    for c in range(8):
        b, h = c // 2, c % 2
        if h == 0:
            xp = np.concatenate([np.zeros((32, D), np.float32), x[b, 0:2048],
                                 np.zeros((96, D), np.float32)], axis=0)
            m0 = m_0h0
        else:
            xp = np.concatenate([x[b, 2016:4096], np.zeros((96, D), np.float32)], axis=0)
            m0 = m_g
        sl = slice(4 * h, 4 * h + 4)
        in_maps.append({
            "x": np.ascontiguousarray(xp),
            "mask0": np.ascontiguousarray(m0),
            "maskg": m_g,
            "wq": wq, "wk": wk, "wv": wv, "wo": wo, "bo": bo,
            "g1": g1, "g2": g2,
            "se": np.ascontiguousarray(se[sl].reshape(S, D)),
            "w1": np.ascontiguousarray(w1[sl]),
            "w2": np.ascontiguousarray(w2[sl]),
            "b1": np.ascontiguousarray(b1[sl].reshape(NE, NHC, 128).transpose(2, 1, 0)),
            "b2": np.ascontiguousarray(b2[sl].reshape(1, NE * D)),
        })
    return in_maps


def kernel(**inputs):
    in_maps = _shard_inputs(inputs)
    nc = build_program()
    res = run_bass_kernel_spmd(nc, in_maps, core_ids=list(range(8)))
    out = np.empty((B, SEQ, D), np.float32)
    for c in range(8):
        b, h = c // 2, c % 2
        out[b, 2048 * h:2048 * (h + 1), :] = np.asarray(res.results[c]["out"])
    return out


# version 16 (final)
# speedup vs baseline: 1.1582x; 1.1582x over previous
import sys
sys.path.insert(0, "/opt/trn_rl_repo")

import numpy as np
from contextlib import ExitStack

import concourse.bass as bass
import concourse.mybir as mybir
from concourse.bass_utils import run_bass_kernel_spmd

F32 = mybir.dt.float32
BF16 = mybir.dt.bfloat16
AF = mybir.ActivationFunctionType
ALU = mybir.AluOpType

B, SEQ, D = 4, 4096, 1024
T = 2048
TK = 2176
H, HD = 16, 64
WND = 160
WPAD = 256
ACOLS = H * WPAD
NTCH = ACOLS // 128
NQB = 16
KD = 8
NT = 16
NTT = 32
S = 2048
NSC = 16
NE = 4
HID = 4096
NHC = 32
SCALE_ATT = 0.125
MASKVAL = -30000.0
GROUPS = [[0, 1], [2, 3], [4, 5], [6, 7]]

DEBUG = False


def build_program(upto=None):
    nc = bass.Bass("TRN2", target_bir_lowering=False, debug=False)

    x_in = nc.dram_tensor("x", [TK, D], F32, kind="ExternalInput").ap()
    mask0_in = nc.dram_tensor("mask0", [128, WND], F32, kind="ExternalInput").ap()
    maskg_in = nc.dram_tensor("maskg", [128, WND], F32, kind="ExternalInput").ap()
    wq_in = nc.dram_tensor("wq", [D, D], F32, kind="ExternalInput").ap()
    wk_in = nc.dram_tensor("wk", [D, D], F32, kind="ExternalInput").ap()
    wv_in = nc.dram_tensor("wv", [D, D], F32, kind="ExternalInput").ap()
    wo_in = nc.dram_tensor("wo", [D, D], F32, kind="ExternalInput").ap()
    bo_in = nc.dram_tensor("bo", [1, D], F32, kind="ExternalInput").ap()
    g1_in = nc.dram_tensor("g1", [128, KD], F32, kind="ExternalInput").ap()
    g2_in = nc.dram_tensor("g2", [128, KD], F32, kind="ExternalInput").ap()
    se_in = nc.dram_tensor("se", [S, D], F32, kind="ExternalInput").ap()
    w1_in = nc.dram_tensor("w1", [NE, D, HID], F32, kind="ExternalInput").ap()
    w2_in = nc.dram_tensor("w2", [NE, HID, D], F32, kind="ExternalInput").ap()
    b1_in = nc.dram_tensor("b1", [128, NHC, NE], F32, kind="ExternalInput").ap()
    b2_in = nc.dram_tensor("b2", [1, NE * D], F32, kind="ExternalInput").ap()

    out_t = nc.dram_tensor("out", [T, D], F32, kind="ExternalOutput").ap()

    xt_d = nc.dram_tensor("xt_d", [TK, D], BF16).ap()
    snt_d = nc.dram_tensor("snt_d", [S, D], BF16).ap()
    raw_d = [nc.dram_tensor(f"raw{i}_d", [SEQ, 1024], F32).ap() for i in range(2)]
    ec_d = nc.dram_tensor("ec_d", [SEQ, S], BF16).ap()
    slots_d = nc.dram_tensor("slots_d", [S, D], BF16).ap()
    y_d = nc.dram_tensor("y_d", [T, D], F32).ap()
    cmrow_d = nc.dram_tensor("cmrow_d", [16, 128], F32).ap()

    ag_in_d = nc.dram_tensor("xn_ag_in", [T, D], BF16)
    ag_out_d = nc.dram_tensor("xn_ag_out", [SEQ, D], BF16)
    mar_in_d = nc.dram_tensor("mar_in", [128, NTT], F32)
    mar_out_d = nc.dram_tensor("mar_out", [128, NTT], F32)
    prs_in_d = nc.dram_tensor("prs_in", [SEQ, 1032], F32)
    prs_out_d = nc.dram_tensor("prs_out", [T, 1032], F32)
    fence_d = [(nc.dram_tensor(f"fi{i}", [128, 8], F32), nc.dram_tensor(f"fo{i}", [128, 8], F32))
               for i in range(3)]

    dbg = {}
    if DEBUG:
        for name, shape, dt in [
            ("dbg_y", [T, D], F32), ("dbg_xnag", [SEQ, D], BF16),
            ("dbg_raw0", [SEQ, 1024], F32), ("dbg_raw1", [SEQ, 1024], F32),
            ("dbg_slots", [S, D], BF16), ("dbg_ec", [SEQ, S], BF16),
            ("dbg_prs", [T, 1032], F32), ("dbg_mar", [128, NTT], F32),
        ]:
            dbg[name] = nc.dram_tensor(name, shape, dt, kind="ExternalOutput").ap()

    ctx = ExitStack()
    with ctx:
        ARENA = 206528
        arena = ctx.enter_context(nc.sbuf_tensor("arena", [128, ARENA], mybir.dt.uint8))
        abase = nc.lookup_mloc(arena).addr

        def ovl(off, shape, dtype, name):
            h = nc.virtual_sbuf_tensor(name, shape, dtype)
            nc._tensor(h.name, shape, dtype, type="SB")
            mloc = nc.lookup_mloc(h)
            mloc.allocated = True
            mloc.addr = abase + off
            per_part = int(np.prod(shape[1:])) * mybir.dt.size(dtype)
            assert off + per_part <= ARENA, f"{name}: {off}+{per_part} > {ARENA}"
            return h

        _so = [0]
        def st(shape, dtype, name):
            h = ovl(_so[0], shape, dtype, name)
            per_part = int(np.prod(shape[1:])) * mybir.dt.size(dtype)
            _so[0] = (_so[0] + per_part + 31) & ~31
            return h
        ident_f = st([128, 128], F32, "ident_f")
        ident_b = st([128, 128], BF16, "ident_b")
        ones1 = st([1, 128], F32, "ones1")
        m320_g = st([128, 320], F32, "m320_g")
        m320_0 = st([128, 320], F32, "m320_0")
        boB = st([128, D], F32, "boB")
        b2Ball = st([128, NE, D], BF16, "b2Ball")
        g1s = st([128, KD], F32, "g1s")
        g12 = st([128, KD], F32, "g12")
        gA = st([128, KD], F32, "gA")
        gB = st([128, KD], F32, "gB")
        b1pp = st([128, NHC, NE], F32, "b1pp")
        rowmax = st([128, NTT], F32, "rowmax")
        negM = st([128, NTT], F32, "negM")
        cm_pp = st([128, NSC], F32, "cm_pp")
        sqs = st([128, D], F32, "sqs")
        se_stg = st([128, D], F32, "se_stg")
        sn_stg = st([128, D], BF16, "sn_stg")
        stat = st([128, 16], F32, "stat")
        stat2 = st([128, 16], F32, "stat2")
        stat3 = st([128, 16], F32, "stat3")
        recips = st([128, 16], F32, "recips")
        snst = st([128, 16], F32, "snst")
        snst2 = st([128, 16], F32, "snst2")
        snst3 = st([128, 16], F32, "snst3")
        D0 = (_so[0] + 63) & ~63
        assert D0 <= 41000, D0

        x_tm = ovl(D0, [128, 17, 1024], BF16, "x_tm")
        borow = ovl(D0 + 34816, [1, D], F32, "borow")
        b2row = ovl(D0 + 34816 + 4096, [1, NE * D], F32, "b2row")
        negcolB = ovl(D0 + 145920, [128, 2048], BF16, "negcolB")
        cmrow = ovl(D0 + 111104, [1, NSC, 128], F32, "cmrow")
        cm16 = ovl(D0 + 119296, [16, 128], F32, "cm16")
        wq_b = ovl(D0, [128, KD, D], BF16, "wq_b")
        wv_b = ovl(D0, [128, KD, D], BF16, "wv_b")
        A_b = ovl(D0, [128, ACOLS], F32, "A_b")
        AT_b = ovl(D0 + 16384, [128, NTCH, 128], BF16, "AT_b")
        wk_b = ovl(D0 + 16384, [128, KD, D], BF16, "wk_b")
        wo_b = ovl(D0 + 16384, [128, KD, D], BF16, "wo_b")
        x_fm = ovl(D0 + 34816, [128, KD, TK], BF16, "x_fm")
        avfm = ovl(D0 + 34816, [128, KD, T], BF16, "avfm")
        qfm = ovl(D0 + 69632, [128, KD, T], BF16, "qfm")
        kfm = ovl(D0 + 102400, [128, KD, TK], BF16, "kfm")
        vtm = ovl(D0 + 137216, [128, 17, 1024], BF16, "vtm")
        xres = ovl(D0 + 102400, [128, 2, D], F32, "xres")
        ystg = ovl(D0 + 110592, [128, 2, D], F32, "ystg")
        xnstg = ovl(D0 + 118784, [128, 2, D], BF16, "xnstg")
        z_stg = ovl(D0 + 122880, [128, 2, D], F32, "z_stg")
        xn_tm = ovl(D0, [128, NTT, 1032], BF16, "xn_tm")
        sn_fm = ovl(D0 + 66048, [128, KD, S], BF16, "sn_fm")
        cmrun = ovl(D0 + 98816, [128, 2048], F32, "cmrun")
        xf_t = ovl(D0 + 107008, [128, 2, KD, 128], BF16, "xf_t")
        raw_stg = ovl(D0 + 111104, [128, 2, 2048], F32, "raw_stg")
        E_q = [ovl(D0 + 66048 + i * 32768, [128, NTT, 512], BF16, f"E_q{i}")
               for i in range(2)]
        raw_rt = ovl(D0 + 150016, [128, 4, 512], F32, "raw_rt")
        tmp_f = ovl(D0 + 158208, [128, 4, 512], F32, "tmp_f")
        ec_stg = ovl(D0 + 166400, [128, 4, 512], BF16, "ec_stg")
        sl_stg = ovl(D0 + 141824, [128, 2, 1024], BF16, "sl_stg")
        wbuf = [ovl(D0 + i * 16384, [128, KD, 1024], BF16, f"wbuf{i}") for i in range(4)]
        slots_fm = ovl(D0 + 65536, [128, KD, S], BF16, "slots_fm")
        h_b = ovl(D0 + 98304, [128, NHC, 512], BF16, "h_b")
        eo_aug = ovl(D0 + 131072, [128, NSC, 1040], BF16, "eo_aug")
        ect = ovl(D0, [128, NSC, T], BF16, "ect")
        pstg = ovl(D0 + 65536, [128, 2, 1040], F32, "pstg")
        ytile = ovl(D0 + 73856, [128, 2, D], F32, "ytile")
        ostg = ovl(D0 + 82048, [128, 2, D], F32, "ostg")

        psA = ctx.enter_context(nc.psum_tensor("psA", [128, 2048], F32))
        psB = ctx.enter_context(nc.psum_tensor("psB", [128, 2048], F32))

        semnames = [
            "d_x", "d_w", "d_c", "d_se", "d_xt", "d_xfm", "d_sn", "d_snf",
            "d_xres", "d_y", "d_xnw", "d_xnt", "d_xf0", "d_xf1", "d_raww", "d_mar",
            "d_rawr", "d_ecw", "d_slw", "d_slf", "d_ect", "d_prw", "d_pro",
            "d_yr", "d_out", "d_wb0", "d_wb1", "d_wb2", "d_wb3",
            "c_prep", "c_gid",
            "t_bb", "v_bb",
            "t_qk", "v_qk", "t_v", "v_v",
            "t_sc", "v_msk", "s_exp", "v_nrm", "t_tr", "v_at", "t_av", "v_av",
            "t_wo", "v_ln1", "s_ln1",
            "s_snB", "v_snA", "s_snD", "v_snC",
            "cc", "t_lg", "s_lg", "v_lg",
            "t_cm", "v_cm", "t_cb", "v_cb", "d_cm",
            "v_ed", "s_ed", "t_sl", "v_sl",
            "t_h", "s_h", "t_w2", "v_eo",
            "t_p", "s_p", "v_ep", "s_ep",
            "v_misc", "s_misc",
        ]
        SM = {n: ctx.enter_context(nc.semaphore(n)) for n in semnames}

        block = ctx.enter_context(nc.Block())


        def p0_setup(which, eng):
            if which == "sync":
                eng.dma_start(out=m320_g[:, 0:160], in_=maskg_in).then_inc(SM["d_c"], 16)
                eng.dma_start(out=m320_g[:, 160:320], in_=maskg_in).then_inc(SM["d_c"], 16)
                eng.dma_start(out=m320_0[:, 0:160], in_=mask0_in).then_inc(SM["d_c"], 16)
                eng.dma_start(out=m320_0[:, 160:320], in_=mask0_in).then_inc(SM["d_c"], 16)
                eng.dma_start(out=borow[:], in_=bo_in).then_inc(SM["d_c"], 16)
                eng.dma_start(out=gA[:], in_=g1_in).then_inc(SM["d_c"], 16)
                eng.dma_start(out=gB[:], in_=g2_in).then_inc(SM["d_c"], 16)
                eng.dma_start(out=b1pp[:], in_=b1_in).then_inc(SM["d_c"], 16)
                eng.dma_start(out=b2row[:], in_=b2_in).then_inc(SM["d_c"], 16)
                eng.wait_ge(SM["d_c"], 144)
                for fi, _fo in fence_d:
                    eng.dma_start(out=fi.ap(), in_=m320_g[:, 0:8]).then_inc(SM["d_c"], 16)
                eng.wait_ge(SM["d_x"], 32)
                eng.dma_start(out=xt_d.rearrange("(n p) d -> p n d", p=128), in_=x_tm[:]
                              ).then_inc(SM["d_xt"], 16)
            if which == "gpsimd":
                eng.memset(ident_f[:], 0.0).then_inc(SM["c_gid"], 1)
                eng.memset(ident_b[:], 0.0).then_inc(SM["c_gid"], 1)
                eng.memset(ones1[:], 1.0).then_inc(SM["c_gid"], 1)
                eng.wait_ge(SM["c_gid"], 2)
                eng.affine_select(out=ident_f[:], in_=ident_f[:], compare_op=ALU.not_equal,
                                  fill=1.0, base=0, pattern=[[-1, 128]],
                                  channel_multiplier=1).then_inc(SM["c_prep"], 1)
                eng.affine_select(out=ident_b[:], in_=ident_b[:], compare_op=ALU.not_equal,
                                  fill=1.0, base=0, pattern=[[-1, 128]],
                                  channel_multiplier=1).then_inc(SM["c_prep"], 1)
                eng.dma_start(out=x_tm[:, 0:9, :], in_=x_in[0:1152, :].rearrange(
                    "(n p) d -> p n d", p=128)).then_inc(SM["d_x"], 16)
                eng.dma_start(out=x_tm[:, 9:17, :], in_=x_in[1152:2176, :].rearrange(
                    "(n p) d -> p n d", p=128)).then_inc(SM["d_x"], 16)
                eng.wait_ge(SM["d_xt"], 16)
                eng.dma_start(out=wq_b[:], in_=wq_in.rearrange("(k p) d -> p k d", p=128)
                              ).then_inc(SM["d_w"], 16)
                eng.dma_start(out=wk_b[:], in_=wk_in.rearrange("(k p) d -> p k d", p=128)
                              ).then_inc(SM["d_w"], 16)
            if which == "scalar":
                eng.wait_ge(SM["d_xt"], 16)
                eng.wait_ge(SM["t_bb"], 2)
                for c in range(KD):
                    eng.dma_start(out=x_fm[:, c, :], in_=xt_d[:, 128 * c:128 * (c + 1)],
                                  transpose=True).then_inc(SM["d_xfm"], 16)
            if which == "vector":
                eng.wait_ge(SM["d_c"], 144)
                eng.tensor_scalar(out=g1s[:], in0=gA[:], scalar1=32.0, scalar2=None, op0=ALU.mult)
                eng.tensor_tensor(out=g12[:], in0=gA[:], in1=gB[:], op=ALU.mult)
                eng.drain()
                eng.tensor_scalar(out=g12[:], in0=g12[:], scalar1=1024.0, scalar2=None,
                                  op0=ALU.mult)
                eng.drain()
                eng.nop().then_inc(SM["v_misc"], 1)
            if which == "tensor":
                eng.wait_ge(SM["c_gid"], 3)
                eng.wait_ge(SM["d_c"], 144)
                for j in range(2):
                    eng.matmul(psA[:, 512 * j:512 * (j + 1)], ones1[:],
                               borow[:, 512 * j:512 * (j + 1)], start=True, stop=True)
                for e in range(3):
                    reg = psA if e == 0 else psB
                    base = 1024 if e == 0 else 1024 * (e - 1)
                    for j in range(2):
                        mm = eng.matmul(reg[:, base + 512 * j:base + 512 * (j + 1)], ones1[:],
                                        b2row[:, 1024 * e + 512 * j:1024 * e + 512 * (j + 1)],
                                        start=True, stop=True)
                mm.then_inc(SM["t_bb"], 1)
                eng.wait_ge(SM["v_bb"], 1)
                for j in range(2):
                    mm = eng.matmul(psA[:, 512 * j:512 * (j + 1)], ones1[:],
                                    b2row[:, 3072 + 512 * j:3072 + 512 * (j + 1)],
                                    start=True, stop=True)
                mm.then_inc(SM["t_bb"], 1)

        def p0b_bcast_evac(which, eng):
            if which == "vector":
                eng.wait_ge(SM["t_bb"], 1)
                eng.tensor_copy(boB[:], psA[:, 0:1024])
                eng.tensor_copy(b2Ball[:, 0, :], psA[:, 1024:2048])
                eng.tensor_copy(b2Ball[:, 1, :], psB[:, 0:1024])
                eng.tensor_copy(b2Ball[:, 2, :], psB[:, 1024:2048]).then_inc(SM["v_bb"], 1)
                eng.wait_ge(SM["t_bb"], 2)
                eng.tensor_copy(b2Ball[:, 3, :], psA[:, 0:1024]).then_inc(SM["v_bb"], 1)

        def p1_qkv(which, eng):
            KCH = [(0, 512), (512, 512), (1024, 512), (1536, 512), (2048, 128)]
            if which == "tensor":
                eng.wait_ge(SM["v_bb"], 2)
                eng.wait_ge(SM["d_w"], 32)
                eng.wait_ge(SM["d_xfm"], 128)
                n = 0
                for do in range(KD):
                    for j in range(4):
                        ps = psA if n % 2 == 0 else psB
                        if n >= 2:
                            eng.wait_ge(SM["v_qk"], n - 1)
                        for k in range(KD):
                            mm = eng.matmul(ps[:, 0:512], wq_b[:, k, 128 * do:128 * (do + 1)],
                                            x_fm[:, k, 32 + 512 * j:32 + 512 * (j + 1)],
                                            start=(k == 0), stop=(k == KD - 1))
                        mm.then_inc(SM["t_qk"], 1)
                        n += 1
                for do in range(KD):
                    for (f0, fl) in KCH:
                        ps = psA if n % 2 == 0 else psB
                        eng.wait_ge(SM["v_qk"], n - 1)
                        for k in range(KD):
                            mm = eng.matmul(ps[:, 0:fl], wk_b[:, k, 128 * do:128 * (do + 1)],
                                            x_fm[:, k, f0:f0 + fl],
                                            start=(k == 0), stop=(k == KD - 1))
                        mm.then_inc(SM["t_qk"], 1)
                        n += 1
                eng.wait_ge(SM["d_w"], 48)
                for tn in range(17):
                    ps = psA if tn % 2 == 0 else psB
                    if tn >= 2:
                        eng.wait_ge(SM["v_v"], tn - 1)
                    for k in range(KD):
                        for j in range(2):
                            mm = eng.matmul(ps[:, 1024 + 512 * j:1024 + 512 * (j + 1)],
                                            x_fm[:, k, 128 * tn:128 * (tn + 1)],
                                            wv_b[:, k, 512 * j:512 * (j + 1)],
                                            start=(k == 0), stop=(k == KD - 1))
                    mm.then_inc(SM["t_v"], 1)
            if which == "vector":
                n = 0
                for do in range(KD):
                    for j in range(4):
                        ps = psA if n % 2 == 0 else psB
                        eng.wait_ge(SM["t_qk"], n + 1)
                        eng.tensor_scalar(out=qfm[:, do, 512 * j:512 * (j + 1)],
                                          in0=ps[:, 0:512], scalar1=SCALE_ATT, scalar2=None,
                                          op0=ALU.mult).then_inc(SM["v_qk"], 1)
                        n += 1
                for do in range(KD):
                    for (f0, fl) in [(0, 512), (512, 512), (1024, 512), (1536, 512), (2048, 128)]:
                        ps = psA if n % 2 == 0 else psB
                        eng.wait_ge(SM["t_qk"], n + 1)
                        eng.tensor_copy(kfm[:, do, f0:f0 + fl],
                                        ps[:, 0:fl]).then_inc(SM["v_qk"], 1)
                        n += 1
                for tn in range(17):
                    ps = psA if tn % 2 == 0 else psB
                    eng.wait_ge(SM["t_v"], tn + 1)
                    eng.tensor_copy(vtm[:, tn, :], ps[:, 1024:2048]).then_inc(SM["v_v"], 1)
                eng.memset(A_b[:], 0.0)
                eng.drain()
                eng.nop().then_inc(SM["v_misc"], 1)
            if which == "gpsimd":
                eng.wait_ge(SM["v_qk"], 32)
                eng.dma_start(out=wv_b[:], in_=wv_in.rearrange("(k p) d -> p k d", p=128)
                              ).then_inc(SM["d_w"], 16)
            if which == "sync":
                for i in range(NSC):
                    if i >= 1:
                        eng.wait_ge(SM["v_snC"], i)
                    eng.dma_start(out=se_stg[:], in_=se_in[128 * i:128 * (i + 1), :]
                                  ).then_inc(SM["d_se"], 16)
                    eng.wait_ge(SM["v_snC"], i + 1)
                    eng.dma_start(out=snt_d[128 * i:128 * (i + 1), :], in_=sn_stg[:]
                                  ).then_inc(SM["d_sn"], 16)
            if which == "scalar":
                for i in range(NSC):
                    eng.wait_ge(SM["d_se"], 16 * (i + 1))
                    eng.activation(sqs[:], se_stg[:], AF.Square,
                                   accum_out=snst[:, i % 16:i % 16 + 1])
                    eng.drain()
                    eng.nop().then_inc(SM["s_snB"], 1)
                    eng.wait_ge(SM["v_snA"], i + 1)
                    eng.activation(snst3[:, i % 16:i % 16 + 1], snst2[:, i % 16:i % 16 + 1],
                                   AF.Sqrt)
                    eng.drain()
                    eng.nop().then_inc(SM["s_snD"], 1)

        def p1_sn_vec(which, eng):
            if which == "vector":
                for i in range(NSC):
                    eng.wait_ge(SM["s_snB"], i + 1)
                    eng.reciprocal(snst2[:, i % 16:i % 16 + 1], snst[:, i % 16:i % 16 + 1])
                    eng.drain()
                    eng.nop().then_inc(SM["v_snA"], 1)
                    eng.wait_ge(SM["s_snD"], i + 1)
                    if i >= 1:
                        eng.wait_ge(SM["d_sn"], 16 * i)
                    eng.tensor_scalar(out=sn_stg[:], in0=se_stg[:],
                                      scalar1=snst3[:, i % 16:i % 16 + 1], scalar2=None,
                                      op0=ALU.mult)
                    eng.drain()
                    eng.nop().then_inc(SM["v_snC"], 1)

        def p2_attn(which, eng):
            def bankap(ps_pair, b, f0, fl):
                ps = ps_pair[0] if b < 4 else ps_pair[1]
                return ps[:, 512 * (b % 4) + f0:512 * (b % 4) + f0 + fl]
            if which == "tensor":
                for qb in range(NQB):
                    for G in range(2):
                        if G == 0:
                            eng.wait_ge(SM["v_av"], qb)
                            if qb == 0:
                                eng.wait_ge(SM["v_misc"], 2)
                        else:
                            eng.wait_ge(SM["s_exp"], 2 * qb + 1)
                        for b in range(8):
                            hh = 8 * G + b
                            po = 64 * (hh % 2)
                            c = hh // 2
                            mm = eng.matmul(bankap((psA, psB), b, 0, WND),
                                            qfm[po:po + 64, c, 128 * qb:128 * (qb + 1)],
                                            kfm[po:po + 64, c, 128 * qb:128 * qb + WND],
                                            start=True, stop=True)
                        mm.then_inc(SM["t_sc"], 1)
                    eng.wait_ge(SM["v_nrm"], qb + 1)
                    for cch in range(NTCH):
                        j = cch % 4
                        if cch >= 4:
                            eng.wait_ge(SM["v_at"], NTCH * qb + cch - 3)
                        eng.transpose(psB[:, 512 * j:512 * j + 128],
                                      A_b[:, 128 * cch:128 * (cch + 1)],
                                      ident_f[:]).then_inc(SM["t_tr"], 1)
                    eng.wait_ge(SM["v_at"], NTCH * (qb + 1))
                    for hh in range(H):
                        c = hh // 2
                        r0 = WPAD * hh
                        segs = []
                        r = r0
                        while r < r0 + WPAD:
                            ch, o = r // 128, r % 128
                            ln = min(128 - o, r0 + WPAD - r)
                            segs.append((ch, o, ln, r - r0))
                            r += ln
                        for si, (ch, o, ln, w0) in enumerate(segs):
                            kt0 = 128 * qb + w0
                            vch, vo = kt0 // 128, kt0 % 128
                            po = 64 * (hh % 2)
                            mm = eng.matmul(
                                (psA if c < 4 else psB)[po:po + 64,
                                                        512 * (c % 4):512 * (c % 4) + 128],
                                vtm[vo:vo + ln, vch, 64 * hh:64 * hh + 64],
                                AT_b[o:o + ln, ch, :],
                                start=(si == 0), stop=(si == len(segs) - 1))
                    mm.then_inc(SM["t_av"], 1)
            if which == "vector":
                for qb in range(NQB):
                    m = m320_0 if qb == 0 else m320_g
                    for G in range(2):
                        eng.wait_ge(SM["t_sc"], 2 * qb + G + 1)
                        for b in range(8):
                            eng.tensor_tensor(out=bankap((psA, psB), b, 0, WND),
                                              in0=bankap((psA, psB), b, 0, WND),
                                              in1=m[:, 0:WND], op=ALU.add)
                        eng.drain()
                        for b in range(8):
                            eng.tensor_reduce(stat[:, b:b + 1],
                                              bankap((psA, psB), b, 0, WND),
                                              axis=mybir.AxisListType.X, op=ALU.max)
                        eng.drain()
                        eng.tensor_reduce(stat2[:, G:G + 1], stat[:, 0:8],
                                          axis=mybir.AxisListType.X,
                                          op=ALU.max, negate=True)
                        eng.drain()
                        eng.nop().then_inc(SM["v_msk"], 1)
                    eng.wait_ge(SM["s_exp"], 2 * qb + 2)
                    eng.tensor_reduce(stat3[:, 0:16],
                                      A_b[:].rearrange("p (h w) -> p h w", w=WPAD),
                                      axis=mybir.AxisListType.X, op=ALU.add)
                    eng.drain()
                    eng.reciprocal(recips[:, 0:16], stat3[:, 0:16])
                    eng.drain()
                    for hh in range(H):
                        eng.tensor_scalar(out=A_b[:, WPAD * hh:WPAD * hh + WND],
                                          in0=A_b[:, WPAD * hh:WPAD * hh + WND],
                                          scalar1=recips[:, hh:hh + 1], scalar2=None,
                                          op0=ALU.mult)
                    eng.drain()
                    eng.nop().then_inc(SM["v_nrm"], 1)
                    for cch in range(NTCH):
                        j = cch % 4
                        eng.wait_ge(SM["t_tr"], NTCH * qb + cch + 1)
                        eng.tensor_copy(AT_b[:, cch, :],
                                        psB[:, 512 * j:512 * j + 128]).then_inc(SM["v_at"], 1)
                    eng.wait_ge(SM["t_av"], qb + 1)
                    for hh in range(H):
                        c = hh // 2
                        po = 64 * (hh % 2)
                        eng.tensor_copy(avfm[po:po + 64, c, 128 * qb:128 * (qb + 1)],
                                        (psA if c < 4 else psB)[po:po + 64,
                                                                512 * (c % 4):512 * (c % 4) + 128])
                    eng.drain()
                    eng.nop().then_inc(SM["v_av"], 1)
            if which == "scalar":
                for qb in range(NQB):
                    A3 = A_b[:].rearrange("p (h w) -> p h w", w=WPAD)
                    for G in range(2):
                        eng.wait_ge(SM["v_msk"], 2 * qb + G + 1)
                        for b in range(8):
                            eng.activation(A3[:, 8 * G + b, 0:WND],
                                           bankap((psA, psB), b, 0, WND),
                                           AF.Exp, bias=stat2[:, G:G + 1], scale=1.0)
                        eng.drain()
                        eng.nop().then_inc(SM["s_exp"], 1)

        def p3_ln1(which, eng):
            if which == "gpsimd":
                eng.wait_ge(SM["v_av"], NQB)
                eng.dma_start(out=wo_b[:], in_=wo_in.rearrange("(k p) d -> p k d", p=128)
                              ).then_inc(SM["d_w"], 16)
            if which == "sync":
                for t in range(NT):
                    if t == 0:
                        eng.wait_ge(SM["t_sc"], 2 * NQB)
                    if t >= 2:
                        eng.wait_ge(SM["v_ln1"], 2 * (t - 1))
                    eng.dma_start(out=xres[:, t % 2, :],
                                  in_=x_in[32 + 128 * t:32 + 128 * (t + 1), :]
                                  ).then_inc(SM["d_xres"], 16)
                    eng.wait_ge(SM["s_misc"], t + 1)
                    eng.dma_start(out=y_d[128 * t:128 * (t + 1), :], in_=ystg[:, t % 2, :]
                                  ).then_inc(SM["d_y"], 16)
                    eng.dma_start(out=ag_in_d.ap()[128 * t:128 * (t + 1), :],
                                  in_=xnstg[:, t % 2, :]).then_inc(SM["d_xnw"], 16)
            if which == "tensor":
                eng.wait_ge(SM["d_w"], 64)
                for t in range(NT):
                    ps = psA if t % 2 == 0 else psB
                    if t >= 2:
                        eng.wait_ge(SM["v_ln1"], 2 * (t - 1))
                    for k in range(KD):
                        for j in range(2):
                            mm = eng.matmul(ps[:, 512 * j:512 * (j + 1)],
                                            avfm[:, k, 128 * t:128 * (t + 1)],
                                            wo_b[:, k, 512 * j:512 * (j + 1)],
                                            start=(k == 0), stop=(k == KD - 1))
                    mm.then_inc(SM["t_wo"], 1)
            if which == "vector":
                for t in range(NT):
                    ps = psA if t % 2 == 0 else psB
                    eng.wait_ge(SM["t_wo"], t + 1)
                    eng.wait_ge(SM["d_xres"], 16 * (t + 1))
                    if t >= 2:
                        eng.wait_ge(SM["s_misc"], t - 1)
                    eng.tensor_tensor(out=z_stg[:, t % 2, :], in0=ps[:, 0:1024],
                                      in1=xres[:, t % 2, :], op=ALU.add)
                    eng.drain()
                    eng.tensor_tensor(out=z_stg[:, t % 2, :], in0=z_stg[:, t % 2, :],
                                      in1=boB[:], op=ALU.add)
                    eng.drain()
                    eng.tensor_reduce(stat[:, 0:1], z_stg[:, t % 2, :],
                                      axis=mybir.AxisListType.X, op=ALU.add)
                    eng.drain()
                    eng.tensor_scalar(out=stat[:, 1:2], in0=stat[:, 0:1], scalar1=1.0 / D,
                                      scalar2=None, op0=ALU.mult)
                    eng.drain()
                    eng.tensor_scalar(out=z_stg[:, t % 2, :], in0=z_stg[:, t % 2, :],
                                      scalar1=stat[:, 1:2], scalar2=None, op0=ALU.subtract)
                    eng.drain()
                    eng.nop().then_inc(SM["v_ln1"], 1)
                    eng.wait_ge(SM["s_ln1"], t + 1)
                    eng.tensor_scalar(out=stat[:, 3:4], in0=stat[:, 2:3], scalar1=1.0 / D,
                                      scalar2=1e-5, op0=ALU.mult, op1=ALU.add)
                    eng.drain()
                    eng.reciprocal(stat[:, 4:5], stat[:, 3:4])
                    eng.reciprocal(stat[:, 5:6], stat[:, 2:3])
                    eng.drain()
                    eng.nop().then_inc(SM["v_ln1"], 1)
            if which == "scalar":
                for t in range(NT):
                    eng.wait_ge(SM["v_ln1"], 2 * t + 1)
                    eng.activation(sqs[:], z_stg[:, t % 2, :], AF.Square,
                                   accum_out=stat[:, 2:3])
                    eng.drain()
                    eng.nop().then_inc(SM["s_ln1"], 1)
                    eng.wait_ge(SM["v_ln1"], 2 * t + 2)
                    if t >= 2:
                        eng.wait_ge(SM["d_y"], 16 * t)
                        eng.wait_ge(SM["d_xnw"], 16 * t)
                    eng.activation(stat[:, 6:7], stat[:, 4:5], AF.Sqrt)
                    eng.activation(stat[:, 7:8], stat[:, 5:6], AF.Sqrt)
                    eng.drain()
                    eng.activation(ystg[:, t % 2, :], z_stg[:, t % 2, :], AF.Copy,
                                   scale=stat[:, 6:7])
                    eng.activation(xnstg[:, t % 2, :], z_stg[:, t % 2, :], AF.Copy,
                                   scale=stat[:, 7:8])
                    eng.drain()
                    eng.nop().then_inc(SM["s_misc"], 1)

        def p4_ag(which, eng):
            if which == "gpsimd":
                eng.wait_ge(SM["d_xnw"], 16 * NT)
                eng.wait_ge(SM["d_y"], 16 * NT)
                eng.wait_ge(SM["d_xres"], 16 * NT)
                eng.wait_ge(SM["d_c"], 192)
                eng.collective_compute("AllGather", ALU.bypass, replica_groups=GROUPS,
                                       ins=[ag_in_d.ap().opt()], outs=[ag_out_d.ap().opt()]
                                       ).then_inc(SM["cc"], 1)
                eng.collective_compute("AllReduce", ALU.add, replica_groups=GROUPS,
                                       ins=[fence_d[0][0].ap().opt()],
                                       outs=[fence_d[0][1].ap().opt()]).then_inc(SM["cc"], 1)
            if which == "sync":
                eng.wait_ge(SM["cc"], 2)
                for g4 in range(4):
                    eng.dma_start(out=xn_tm[:, 8 * g4:8 * (g4 + 1), 0:1024],
                                  in_=ag_out_d.ap()[1024 * g4:1024 * (g4 + 1), :].rearrange(
                                      "(n p) d -> p n d", p=128)).then_inc(SM["d_xnt"], 16)
            if which == "scalar":
                eng.wait_ge(SM["d_sn"], 16 * NSC)
                eng.wait_ge(SM["d_xnw"], 16 * NT)
                for c in range(KD):
                    eng.dma_start(out=sn_fm[:, c, :], in_=snt_d[:, 128 * c:128 * (c + 1)],
                                  transpose=True).then_inc(SM["d_snf"], 16)
            if which == "vector":
                eng.wait_ge(SM["d_xnt"], 64)
                eng.memset(xn_tm[:, :, 1024:1025], 1.0)
                eng.memset(xn_tm[:, :, 1025:1032], 0.0)
                eng.drain()
                eng.nop().then_inc(SM["v_misc"], 1)
                eng.wait_ge(SM["d_snf"], 128)
                for c in range(KD):
                    eng.tensor_scalar(out=sn_fm[:, c, :], in0=sn_fm[:, c, :],
                                      scalar1=g12[:, c:c + 1], scalar2=None, op0=ALU.mult)
                eng.drain()
                eng.nop().then_inc(SM["v_misc"], 1)

        def p5_logits(which, eng):
            if which == "scalar":
                for t in range(NTT + 2):
                    if t >= 2:
                        tt = t - 2
                        ps = psA if tt % 2 == 0 else psB
                        eng.wait_ge(SM["t_lg"], tt + 1)
                        if tt >= 2:
                            eng.wait_ge(SM["d_raww"], 32 * tt)
                            eng.wait_ge(SM["v_lg"], 2 * (tt - 1))
                        eng.copy(raw_stg[:, tt % 2, :], ps[:, :])
                        eng.drain()
                        eng.nop().then_inc(SM["s_lg"], 1)
                    if t < NTT:
                        eng.wait_ge(SM["cc"], 2)
                        if t >= 2:
                            eng.wait_ge(SM["t_lg"], t - 1)
                        for c in range(KD):
                            eng.dma_start(out=xf_t[:, t % 2, c, :],
                                          in_=ag_out_d.ap()[128 * t:128 * (t + 1),
                                                            128 * c:128 * (c + 1)],
                                          transpose=True).then_inc(SM[f"d_xf{t % 2}"], 16)
            if which == "tensor":
                eng.wait_ge(SM["v_misc"], 4)
                for t in range(NTT):
                    ps = psA if t % 2 == 0 else psB
                    eng.wait_ge(SM[f"d_xf{t % 2}"], 16 * KD * (t // 2 + 1))
                    if t >= 2:
                        eng.wait_ge(SM["s_lg"], t - 1)
                    for k in range(KD):
                        for sc in range(4):
                            mm = eng.matmul(ps[:, 512 * sc:512 * (sc + 1)],
                                            xf_t[:, t % 2, k, :],
                                            sn_fm[:, k, 512 * sc:512 * (sc + 1)],
                                            start=(k == 0), stop=(k == KD - 1))
                    mm.then_inc(SM["t_lg"], 1)
            if which == "sync":
                for t in range(NTT):
                    eng.wait_ge(SM["s_lg"], t + 1)
                    eng.dma_start(out=raw_d[0][128 * t:128 * (t + 1), :],
                                  in_=raw_stg[:, t % 2, 0:1024]).then_inc(SM["d_raww"], 16)
                    eng.dma_start(out=raw_d[1][128 * t:128 * (t + 1), :],
                                  in_=raw_stg[:, t % 2, 1024:2048]).then_inc(SM["d_raww"], 16)
                eng.wait_ge(SM["v_lg"], 2 * NTT)
                eng.dma_start(out=mar_in_d.ap(), in_=rowmax[:]).then_inc(SM["d_mar"], 16)
                eng.wait_ge(SM["v_cm"], NSC + 1)
                eng.dma_start(out=cmrow_d, in_=cm16[:]).then_inc(SM["d_cm"], 16)
                eng.wait_ge(SM["d_cm"], 16)
                eng.dma_start(out=cmrow[:],
                              in_=cmrow_d.rearrange("(x a) b -> x a b", x=1)
                              ).then_inc(SM["d_cm"], 16)
                eng.wait_ge(SM["cc"], 4)
                eng.dma_start(out=negM[:], in_=mar_out_d.ap()).then_inc(SM["d_mar"], 32)
            if which == "vector":
                eng.memset(cmrun[:], -1e30)
                eng.drain()
                for t in range(NTT):
                    eng.wait_ge(SM["s_lg"], t + 1)
                    eng.tensor_reduce(rowmax[:, t:t + 1], raw_stg[:, t % 2, :],
                                      axis=mybir.AxisListType.X, op=ALU.max)
                    eng.drain()
                    eng.nop().then_inc(SM["v_lg"], 1)
                    eng.tensor_tensor(out=cmrun[:], in0=cmrun[:], in1=raw_stg[:, t % 2, :],
                                      op=ALU.max)
                    eng.drain()
                    eng.nop().then_inc(SM["v_lg"], 1)
            if which == "gpsimd":
                eng.wait_ge(SM["d_mar"], 16)
                eng.collective_compute("AllReduce", ALU.max, replica_groups=GROUPS,
                                       ins=[mar_in_d.ap().opt()], outs=[mar_out_d.ap().opt()]
                                       ).then_inc(SM["cc"], 1)
                eng.collective_compute("AllReduce", ALU.add, replica_groups=GROUPS,
                                       ins=[fence_d[1][0].ap().opt()],
                                       outs=[fence_d[1][1].ap().opt()]).then_inc(SM["cc"], 1)

        def p5b_colmax(which, eng):
            if which == "tensor":
                eng.wait_ge(SM["v_lg"], 2 * NTT)
                for c in range(NSC):
                    j = c % 4
                    if c >= 4:
                        eng.wait_ge(SM["v_cm"], c - 3)
                    eng.transpose(psA[:, 512 * j:512 * j + 128],
                                  cmrun[:, 128 * c:128 * (c + 1)],
                                  ident_f[:]).then_inc(SM["t_cm"], 1)
                eng.wait_ge(SM["v_cm"], NSC)
                eng.transpose(psB[0:16, 0:128], cm_pp[:],
                              ident_f[:]).then_inc(SM["t_cm"], 1)
                eng.wait_ge(SM["d_cm"], 32)
                for c in range(NSC):
                    j = c % 4
                    if c >= 4:
                        eng.wait_ge(SM["v_cb"], c - 3)
                    eng.matmul(psA[:, 512 * j:512 * j + 128], ones1[:],
                               cmrow[:, c, :], start=True,
                               stop=True).then_inc(SM["t_cb"], 1)
            if which == "vector":
                for c in range(NSC):
                    j = c % 4
                    eng.wait_ge(SM["t_cm"], c + 1)
                    eng.tensor_reduce(cm_pp[:, c:c + 1], psA[:, 512 * j:512 * j + 128],
                                      axis=mybir.AxisListType.X, op=ALU.max)
                    eng.drain()
                    eng.nop().then_inc(SM["v_cm"], 1)
                eng.wait_ge(SM["t_cm"], NSC + 1)
                eng.wait_ge(SM["d_raww"], 32 * NTT)
                eng.tensor_copy(cm16[:], psB[0:16, 0:128]).then_inc(SM["v_cm"], 1)
                for c in range(NSC):
                    j = c % 4
                    eng.wait_ge(SM["t_cb"], c + 1)
                    eng.tensor_scalar(out=negcolB[:, 128 * c:128 * (c + 1)],
                                      in0=psA[:, 512 * j:512 * j + 128],
                                      scalar1=-1.0, scalar2=None,
                                      op0=ALU.mult).then_inc(SM["v_cb"], 1)
                if True:
                    eng.wait_ge(SM["d_mar"], 48)
                    eng.tensor_scalar(out=negM[:], in0=negM[:], scalar1=-1.0, scalar2=None,
                                      op0=ALU.mult)
                    eng.drain()
                    eng.nop().then_inc(SM["v_misc"], 1)

        def p6_eslots(which, eng):
            RING = 4
            def rd(idx):
                q2, t2 = idx // NTT, idx % NTT
                Hh, qq = q2 // 2, q2 % 2
                eng.wait_ge(SM["d_raww"], 32 * (t2 + 1))
                eng.dma_start(out=raw_rt[:, idx % RING, :],
                              in_=raw_d[Hh][128 * t2:128 * (t2 + 1),
                                            512 * qq:512 * (qq + 1)]
                              ).then_inc(SM["d_rawr"], 16)
            if which == "sync":
                for i in range(3):
                    rd(i)
                for q in range(4):
                    for t in range(NTT):
                        idx = q * NTT + t
                        if idx + 3 < 4 * NTT:
                            if idx >= 1:
                                eng.wait_ge(SM["s_ed"], 2 * idx)
                            rd(idx + 3)
                        eng.wait_ge(SM["s_ed"], 2 * idx + 2)
                        eng.dma_start(out=ec_d[128 * t:128 * (t + 1), 512 * q:512 * (q + 1)],
                                      in_=ec_stg[:, idx % RING, :]).then_inc(SM["d_ecw"], 16)
                    for scc in range(4):
                        g = 4 * q + scc
                        eng.wait_ge(SM["v_sl"], g + 1)
                        eng.dma_start(out=slots_d[128 * g:128 * (g + 1), :],
                                      in_=sl_stg[:, g % 2, :]).then_inc(SM["d_slw"], 16)
            if which == "vector":
                eng.wait_ge(SM["v_cb"], NSC)
                for q in range(4):
                    for t in range(NTT):
                        idx = q * NTT + t
                        eng.wait_ge(SM["d_rawr"], 16 * (idx + 1))
                        if idx >= RING:
                            eng.wait_ge(SM["s_ed"], 2 * (idx - RING) + 1)
                        eng.scalar_tensor_tensor(out=tmp_f[:, idx % RING, :],
                                                 in0=raw_rt[:, idx % RING, :], scalar=1.0,
                                                 in1=negcolB[:, 512 * q:512 * (q + 1)],
                                                 op0=ALU.mult, op1=ALU.add)
                        eng.drain()
                        eng.nop().then_inc(SM["v_ed"], 1)
                    for scc in range(4):
                        g = 4 * q + scc
                        ps = psA if g % 2 == 0 else psB
                        eng.wait_ge(SM["t_sl"], g + 1)
                        if g >= 2:
                            eng.wait_ge(SM["d_slw"], 16 * g)
                        eng.reciprocal(stat2[:, 1:2], ps[:, 1024:1025])
                        eng.drain()
                        eng.tensor_scalar(out=sl_stg[:, g % 2, :], in0=ps[:, 0:1024],
                                          scalar1=stat2[:, 1:2], scalar2=None, op0=ALU.mult)
                        eng.drain()
                        eng.nop().then_inc(SM["v_sl"], 1)
            if which == "scalar":
                for q in range(4):
                    if q >= 2:
                        eng.wait_ge(SM["t_sl"], 4 * (q - 1))
                    for t in range(NTT):
                        idx = q * NTT + t
                        eng.wait_ge(SM["v_ed"], idx + 1)
                        eng.activation(E_q[q % 2][:, t, :], tmp_f[:, idx % RING, :],
                                       AF.Exp).then_inc(SM["s_ed"], 1)
                        eng.wait_ge(SM["v_misc"], 5)
                        if idx >= RING:
                            eng.wait_ge(SM["d_ecw"], 16 * (idx - RING + 1))
                        eng.activation(ec_stg[:, idx % RING, :], raw_rt[:, idx % RING, :],
                                       AF.Exp, bias=negM[:, t:t + 1], scale=1.0)
                        eng.drain()
                        eng.nop().then_inc(SM["s_ed"], 1)
            if which == "tensor":
                eng.wait_ge(SM["v_misc"], 3)
                for q in range(4):
                    for scc in range(4):
                        g = 4 * q + scc
                        ps = psA if g % 2 == 0 else psB
                        if g >= 2:
                            eng.wait_ge(SM["v_sl"], g - 1)
                        for t in range(NTT):
                            idx = q * NTT + t
                            eng.wait_ge(SM["s_ed"], 2 * idx + 1)
                            for (f0, fl) in [(0, 512), (512, 512), (1024, 8)]:
                                mm = eng.matmul(ps[:, f0:f0 + fl],
                                                E_q[q % 2][:, t, 128 * scc:128 * (scc + 1)],
                                                xn_tm[:, t, f0:f0 + fl],
                                                start=(t == 0), stop=(t == NTT - 1))
                        mm.then_inc(SM["t_sl"], 1)
        def p7_ffn(which, eng):
            def wl(i):
                e, r = divmod(i, 12)
                if r < 4:
                    return True, e, 0, r
                sp, jq = divmod(r - 4, 4)
                return False, e, sp, jq
            NL = 12 * NE
            if which == "gpsimd":
                eng.wait_ge(SM["t_sl"], NSC)
                for i in range(NL):
                    isw1, e, sp, jq = wl(i)
                    if i >= 4:
                        pw1, pe, psp, pjq = wl(i - 4)
                        if pw1:
                            eng.wait_ge(SM["t_h"], NHC * pe + 8 * (pjq + 1))
                        else:
                            eng.wait_ge(SM["t_w2"], 8 * pe + 4 * psp + pjq + 1)
                    if isw1:
                        src = w1_in[e, :, 1024 * jq:1024 * (jq + 1)].rearrange(
                            "(k p) h -> p k h", p=128)
                    else:
                        src = w2_in[e, 1024 * jq:1024 * (jq + 1), :].rearrange(
                            "(k p) d -> p k d", p=128)
                    eng.dma_start(out=wbuf[i % 4][:], in_=src).then_inc(SM[f"d_wb{i % 4}"], 16)
            if which == "scalar":
                eng.wait_ge(SM["d_slw"], 16 * NSC)
                for c in range(KD):
                    eng.dma_start(out=slots_fm[:, c, :], in_=slots_d[:, 128 * c:128 * (c + 1)],
                                  transpose=True).then_inc(SM["d_slf"], 16)
                for e in range(NE):
                    for hc in range(NHC):
                        i = e * NHC + hc
                        eng.wait_ge(SM["t_h"], i + 1)
                        eng.activation(h_b[:, hc, :],
                                       psB[:, 1024 + 512 * (i % 2):1536 + 512 * (i % 2)],
                                       AF.Gelu, bias=b1pp[:, hc, e:e + 1], scale=1.0)
                        eng.drain()
                        eng.nop().then_inc(SM["s_h"], 1)
            if which == "vector":
                eng.wait_ge(SM["d_slf"], 16 * KD)
                eng.wait_ge(SM["v_misc"], 1)
                for c in range(KD):
                    eng.tensor_scalar(out=slots_fm[:, c, :], in0=slots_fm[:, c, :],
                                      scalar1=g1s[:, c:c + 1], scalar2=None, op0=ALU.mult)
                eng.drain()
                eng.wait_ge(SM["d_ecw"], 16 * 4 * NTT)
                eng.wait_ge(SM["d_rawr"], 16 * 4 * NTT)
                eng.memset(eo_aug[:, :, 1024:1040], 0.0)
                eng.drain()
                eng.memset(eo_aug[:, :, 1024:1025], 1.0)
                eng.drain()
                eng.nop().then_inc(SM["v_misc"], 2)
                eng.wait_ge(SM["v_bb"], 2)
                for e in range(NE):
                    for sp in range(2):
                        i = 2 * e + sp
                        eng.wait_ge(SM["t_w2"], 8 * e + 4 * (sp + 1))
                        for sl2 in range(2):
                            sl = 2 * sp + sl2
                            eng.tensor_tensor(out=eo_aug[:, 4 * e + sl, 0:1024],
                                              in0=psA[:, 1024 * sl2:1024 * (sl2 + 1)],
                                              in1=b2Ball[:, e, :], op=ALU.add)
                        eng.drain()
                        eng.nop().then_inc(SM["v_eo"], 1)
            if which == "tensor":
                eng.wait_ge(SM["v_misc"], 7)
                for e in range(NE):
                    if e > 0:
                        eng.wait_ge(SM["v_eo"], 2 * e)
                    else:
                        eng.wait_ge(SM["v_sl"], NSC)
                    for hc in range(NHC):
                        i = e * NHC + hc
                        jq = hc // 8
                        li = e * 12 + jq
                        eng.wait_ge(SM[f"d_wb{li % 4}"], 16 * (li // 4 + 1))
                        if i >= 2:
                            eng.wait_ge(SM["s_h"], i - 1)
                        for k in range(KD):
                            mm = eng.matmul(psB[:, 1024 + 512 * (i % 2):1536 + 512 * (i % 2)],
                                            wbuf[li % 4][:, k, 128 * (hc % 8):128 * (hc % 8 + 1)],
                                            slots_fm[:, k, 512 * e:512 * (e + 1)],
                                            start=(k == 0), stop=(k == KD - 1))
                        mm.then_inc(SM["t_h"], 1)
                    for sp in range(2):
                        i2 = 2 * e + sp
                        eng.wait_ge(SM["v_eo"], i2)
                        if e == 0 and sp == 0:
                            eng.wait_ge(SM["v_sl"], NSC)
                        for jq in range(4):
                            li = e * 12 + 4 + sp * 4 + jq
                            eng.wait_ge(SM[f"d_wb{li % 4}"], 16 * (li // 4 + 1))
                            eng.wait_ge(SM["s_h"], e * NHC + 8 * (jq + 1))
                            for sl2 in range(2):
                                sl = 2 * sp + sl2
                                for k8 in range(KD):
                                    for j in range(2):
                                        mm = eng.matmul(psA[:, 1024 * sl2 + 512 * j:
                                                            1024 * sl2 + 512 * (j + 1)],
                                                        h_b[:, 8 * jq + k8,
                                                            128 * sl:128 * (sl + 1)],
                                                        wbuf[li % 4][:, k8,
                                                                     512 * j:512 * (j + 1)],
                                                        start=(jq == 0 and k8 == 0),
                                                        stop=(jq == 3 and k8 == KD - 1))
                            mm.then_inc(SM["t_w2"], 1)

        def p8_P(which, eng):
            if which == "scalar":
                eng.wait_ge(SM["d_ecw"], 16 * 4 * NTT)
                for half in range(2):
                    eng.wait_ge(SM["t_w2"], 8 * NE)
                    if half == 1:
                        eng.wait_ge(SM["t_p"], 16)
                    for sc in range(NSC):
                        eng.dma_start(out=ect[:, sc, :],
                                      in_=ec_d[2048 * half:2048 * (half + 1),
                                               128 * sc:128 * (sc + 1)],
                                      transpose=True).then_inc(SM["d_ect"], 16)
                    for tt in range(16):
                        t = 16 * half + tt
                        ps = psA if t % 2 == 0 else psB
                        eng.wait_ge(SM["t_p"], t + 1)
                        if t >= 2:
                            eng.wait_ge(SM["d_prw"], 16 * t)
                        eng.copy(pstg[:, t % 2, 0:1040], ps[:, 0:1040])
                        eng.drain()
                        eng.nop().then_inc(SM["s_p"], 1)
            if which == "tensor":
                eng.wait_ge(SM["v_eo"], 2 * NE)
                for half in range(2):
                    eng.wait_ge(SM["d_ect"], 16 * NSC * (half + 1))
                    for tt in range(16):
                        t = 16 * half + tt
                        ps = psA if t % 2 == 0 else psB
                        if t >= 2:
                            eng.wait_ge(SM["s_p"], t - 1)
                        for sc in range(NSC):
                            for (f0, fl) in [(0, 512), (512, 512), (1024, 16)]:
                                mm = eng.matmul(ps[:, f0:f0 + fl],
                                                ect[:, sc, 128 * tt:128 * (tt + 1)],
                                                eo_aug[:, sc, f0:f0 + fl],
                                                start=(sc == 0), stop=(sc == NSC - 1))
                        mm.then_inc(SM["t_p"], 1)
            if which == "sync":
                for t in range(NTT):
                    eng.wait_ge(SM["s_p"], t + 1)
                    eng.dma_start(out=prs_in_d.ap()[128 * t:128 * (t + 1), :],
                                  in_=pstg[:, t % 2, 0:1032]).then_inc(SM["d_prw"], 16)
            if which == "gpsimd":
                eng.wait_ge(SM["d_prw"], 16 * NTT)
                eng.collective_compute("ReduceScatter", ALU.add, replica_groups=GROUPS,
                                       ins=[prs_in_d.ap().opt()], outs=[prs_out_d.ap().opt()]
                                       ).then_inc(SM["cc"], 1)
                eng.collective_compute("AllReduce", ALU.add, replica_groups=GROUPS,
                                       ins=[fence_d[2][0].ap().opt()],
                                       outs=[fence_d[2][1].ap().opt()]).then_inc(SM["cc"], 1)

        def p9_epi(which, eng):
            if which == "sync":
                eng.wait_ge(SM["cc"], 6)
                for t in range(NT):
                    if t >= 2:
                        eng.wait_ge(SM["v_ep"], 2 * (t - 1))
                    eng.dma_start(out=pstg[:, t % 2, 0:1032],
                                  in_=prs_out_d.ap()[128 * t:128 * (t + 1), :]
                                  ).then_inc(SM["d_pro"], 16)
                    eng.dma_start(out=ytile[:, t % 2, :], in_=y_d[128 * t:128 * (t + 1), :]
                                  ).then_inc(SM["d_yr"], 16)
                    eng.wait_ge(SM["s_ep"], t + 1)
                    eng.dma_start(out=out_t[128 * t:128 * (t + 1), :], in_=ostg[:, t % 2, :]
                                  ).then_inc(SM["d_out"], 16)
                nout = 16 * NT
                if DEBUG:
                    eng.wait_ge(SM["d_out"], nout)
                    for nm, src in [("dbg_y", y_d), ("dbg_xnag", ag_out_d.ap()),
                                    ("dbg_raw0", raw_d[0]), ("dbg_raw1", raw_d[1]),
                                    ("dbg_slots", slots_d), ("dbg_ec", ec_d),
                                    ("dbg_prs", prs_out_d.ap()), ("dbg_mar", mar_out_d.ap())]:
                        eng.dma_start(out=dbg[nm], in_=src).then_inc(SM["d_out"], 16)
                    nout += 16 * 8
                eng.wait_ge(SM["d_out"], nout)
            if which == "vector":
                for t in range(NT):
                    eng.wait_ge(SM["d_pro"], 16 * (t + 1))
                    eng.wait_ge(SM["d_yr"], 16 * (t + 1))
                    if t >= 2:
                        eng.wait_ge(SM["s_misc"], NT + 2 * (t - 1))
                    eng.reciprocal(stat2[:, 2:3], pstg[:, t % 2, 1024:1025])
                    eng.drain()
                    eng.scalar_tensor_tensor(out=z_stg[:, t % 2, :],
                                             in0=pstg[:, t % 2, 0:1024],
                                             scalar=stat2[:, 2:3], in1=ytile[:, t % 2, :],
                                             op0=ALU.mult, op1=ALU.add)
                    eng.drain()
                    eng.tensor_reduce(stat[:, 8:9], z_stg[:, t % 2, :],
                                      axis=mybir.AxisListType.X, op=ALU.add)
                    eng.drain()
                    eng.tensor_scalar(out=stat[:, 9:10], in0=stat[:, 8:9], scalar1=1.0 / D,
                                      scalar2=None, op0=ALU.mult)
                    eng.drain()
                    eng.tensor_scalar(out=z_stg[:, t % 2, :], in0=z_stg[:, t % 2, :],
                                      scalar1=stat[:, 9:10], scalar2=None, op0=ALU.subtract)
                    eng.drain()
                    eng.nop().then_inc(SM["v_ep"], 1)
                    eng.wait_ge(SM["s_misc"], NT + 2 * t + 1)
                    eng.tensor_scalar(out=stat[:, 11:12], in0=stat[:, 10:11], scalar1=1.0 / D,
                                      scalar2=1e-5, op0=ALU.mult, op1=ALU.add)
                    eng.drain()
                    eng.reciprocal(stat[:, 12:13], stat[:, 11:12])
                    eng.drain()
                    eng.nop().then_inc(SM["v_ep"], 1)
            if which == "scalar":
                for t in range(NT):
                    eng.wait_ge(SM["v_ep"], 2 * t + 1)
                    eng.activation(sqs[:], z_stg[:, t % 2, :], AF.Square,
                                   accum_out=stat[:, 10:11])
                    eng.drain()
                    eng.nop().then_inc(SM["s_misc"], 1)
                    eng.wait_ge(SM["v_ep"], 2 * t + 2)
                    if t >= 2:
                        eng.wait_ge(SM["d_out"], 16 * t)
                    eng.activation(stat[:, 13:14], stat[:, 12:13], AF.Sqrt)
                    eng.drain()
                    eng.activation(ostg[:, t % 2, :], z_stg[:, t % 2, :], AF.Copy,
                                   scale=stat[:, 13:14])
                    eng.drain()
                    eng.nop().then_inc(SM["s_ep"], 1)
                    eng.nop().then_inc(SM["s_misc"], 1)

        PHASES = [p0_setup, p0b_bcast_evac, p1_qkv, p1_sn_vec, p2_attn,
                  p3_ln1, p4_ag, p5_logits, p5b_colmax, p6_eslots, p7_ffn,
                  p8_P, p9_epi]
        if upto is not None:
            PHASES = PHASES[:upto]

            def p_dbg_out(which, eng):
                if which == "sync":
                    for t in range(NT):
                        eng.dma_start(out=out_t[128 * t:128 * (t + 1), :],
                                      in_=sqs[:]).then_inc(SM["d_out"], 16)
                    eng.wait_ge(SM["d_out"], 16 * NT)
            PHASES = PHASES + [p_dbg_out]

        @block.tensor
        def _(t):
            for ph in PHASES:
                ph("tensor", t)

        @block.vector
        def _(v):
            for ph in PHASES:
                ph("vector", v)

        @block.scalar
        def _(s):
            for ph in PHASES:
                ph("scalar", s)

        @block.gpsimd
        def _(g):
            for ph in PHASES:
                ph("gpsimd", g)

        @block.sync
        def _(y):
            for ph in PHASES:
                ph("sync", y)

    return nc



def _shard_inputs(inputs):
    x = np.asarray(inputs["L0_attn_x"], np.float32)
    wq = np.ascontiguousarray(inputs["L0_attn_Wq"], dtype=np.float32)
    wk = np.ascontiguousarray(inputs["L0_attn_Wk"], dtype=np.float32)
    wv = np.ascontiguousarray(inputs["L0_attn_Wv"], dtype=np.float32)
    wo = np.ascontiguousarray(inputs["L0_attn_Wo"], dtype=np.float32)
    bo = np.ascontiguousarray(np.asarray(inputs["L0_attn_bo"], np.float32).reshape(1, D))
    g1 = np.ascontiguousarray(np.asarray(inputs["L0_ffn_norm_gamma"], np.float32)
                              .reshape(KD, 128).T)
    g2 = np.ascontiguousarray(np.asarray(inputs["L0_ffn_slot_norm_gamma"], np.float32)
                              .reshape(KD, 128).T)
    se = np.asarray(inputs["L0_ffn_slot_embeds"], np.float32)
    w1 = np.asarray(inputs["L0_ffn_w1"], np.float32)
    w2 = np.asarray(inputs["L0_ffn_w2"], np.float32)
    b1 = np.asarray(inputs["L0_ffn_b1"], np.float32)
    b2 = np.asarray(inputs["L0_ffn_b2"], np.float32)

    ii = np.arange(128)[:, None]
    ww = np.arange(WND)[None, :]
    band = (ww >= ii) & (ww <= ii + 32)
    m_g = np.where(band, 0.0, MASKVAL).astype(np.float32)
    m_0h0 = np.where(band & (ww >= 32), 0.0, MASKVAL).astype(np.float32)

    in_maps = []
    for c in range(8):
        b, h = c // 2, c % 2
        if h == 0:
            xp = np.concatenate([np.zeros((32, D), np.float32), x[b, 0:2048],
                                 np.zeros((96, D), np.float32)], axis=0)
            m0 = m_0h0
        else:
            xp = np.concatenate([x[b, 2016:4096], np.zeros((96, D), np.float32)], axis=0)
            m0 = m_g
        sl = slice(4 * h, 4 * h + 4)
        in_maps.append({
            "x": np.ascontiguousarray(xp),
            "mask0": np.ascontiguousarray(m0),
            "maskg": m_g,
            "wq": wq, "wk": wk, "wv": wv, "wo": wo, "bo": bo,
            "g1": g1, "g2": g2,
            "se": np.ascontiguousarray(se[sl].reshape(S, D)),
            "w1": np.ascontiguousarray(w1[sl]),
            "w2": np.ascontiguousarray(w2[sl]),
            "b1": np.ascontiguousarray(b1[sl].reshape(NE, NHC, 128).transpose(2, 1, 0)),
            "b2": np.ascontiguousarray(b2[sl].reshape(1, NE * D)),
        })
    return in_maps


def kernel(**inputs):
    in_maps = _shard_inputs(inputs)
    nc = build_program()
    res = run_bass_kernel_spmd(nc, in_maps, core_ids=list(range(8)))
    out = np.empty((B, SEQ, D), np.float32)
    for c in range(8):
        b, h = c // 2, c % 2
        out[b, 2048 * h:2048 * (h + 1), :] = np.asarray(res.results[c]["out"])
    return out
